# version 08
# baseline (speedup 1.0000x reference)
import math
import numpy as np
from contextlib import ExitStack

import concourse.bass as bass
import concourse.mybir as mybir
from concourse.bass_utils import run_bass_kernel_spmd
import concourse.bass_utils as _bu

LDW_OPT = True
if not hasattr(_bu, "_orig_run_command"):
    _bu._orig_run_command = _bu.run_command

    def _patched_run_command(cmd, *a, **kw):
        if LDW_OPT and isinstance(cmd, list):
            cmd = ["--enable-ldw-opt=true" if c == "--enable-ldw-opt=false" else c
                   for c in cmd]
        return _bu._orig_run_command(cmd, *a, **kw)

    _bu.run_command = _patched_run_command

F32 = mybir.dt.float32
BF16 = mybir.dt.bfloat16
AF = mybir.ActivationFunctionType
ALU = mybir.AluOpType
AX = mybir.AxisListType

E = 2048
NI = 1408
NEG = -1.0e30
RD = math.sqrt(128.0)
SLOPES = [2.0 ** (-0.5 * (i + 1)) for i in range(16)]

DEBUG = False
PHASES = "ABC"
CPARTS = "WGER"
EXP_N = 8
EXP_STAGE = "full"
SW_MODE = "switch"
AG_MODE = "cc"

ROLES = ("sy", "te", "ve", "se", "gp")

SEMS = ["pe", "dv", "ac", "gq", "sp", "st", "pc", "ag", "x0", "x1", "w0", "w1", "w2", "w3",
        "kv0", "kv1", "qb", "cl", "xr0", "xr1", "in0",
        "s0", "s1", "v0", "v1", "a0", "a1"]


class Proxy:
    def __init__(self, real):
        self._r = real

    def __getattr__(self, name):
        if self._r is None:
            return lambda *a, **k: None
        return getattr(self._r, name)


class Walk:

    def __init__(self, nc, role, sems, cnt):
        self.nc = nc
        self.role = role
        self.sems = sems
        self.cnt = cnt

    def bind(self, eng):
        self.sy = Proxy(eng if self.role == "sy" else None)
        self.te = Proxy(eng if self.role == "te" else None)
        self.ve = Proxy(eng if self.role == "ve" else None)
        self.se = Proxy(eng if self.role == "se" else None)
        self.gp = Proxy(eng if self.role == "gp" else None)

    def inc(self, inst, sem, k=1):
        self.cnt[sem] += k
        if inst is not None:
            inst.then_inc(self.sems[sem], k)
        return self.cnt[sem]

    def wait(self, proxy, sem, val):
        if val > 0:
            proxy.wait_ge(self.sems[sem], val)

    def sw_dma_sy(self, sem, build):
        if self.sy._r is not None:
            eng = self.sy._r
            if SW_MODE == "switch":
                for arm in eng.switch_core_id(n=128):
                    try:
                        lg = int(arm.logical)
                    except Exception:
                        lg = int(arm)
                    build(eng, lg).then_inc(self.sems[sem], 16)
            else:
                build(eng, 0).then_inc(self.sems[sem], 16)
        self.cnt[sem] += 16

    def bar(self):
        for s in SEMS:
            if self.cnt[s] > 0:
                for p in (self.sy, self.te, self.ve, self.se, self.gp):
                    if p._r is not None:
                        p.wait_ge(self.sems[s], self.cnt[s])


def build_program():
    nc = bass.Bass("TRN2", target_bir_lowering=False)

    d_xt = nc.dram_tensor("xt", [32, 2048, 128], F32, kind="ExternalInput").ap()
    d_bt = nc.dram_tensor("btab", [8, 32, 128, 512], F32, kind="ExternalInput").ap()
    d_wq = nc.dram_tensor("wq", [8, 2048, 128], F32, kind="ExternalInput").ap()
    d_wk = nc.dram_tensor("wk", [8, 2048, 128], F32, kind="ExternalInput").ap()
    d_wv = nc.dram_tensor("wv", [8, 2048, 128], F32, kind="ExternalInput").ap()
    d_xo = nc.dram_tensor("xo", [16, 2048, 128], F32, kind="ExternalInput").ap()
    d_slp = nc.dram_tensor("slp", [128, 8], F32, kind="ExternalInput").ap()
    d_wo = nc.dram_tensor("wo", [16, 2048, 128], F32, kind="ExternalInput").ap()
    d_w1 = nc.dram_tensor("w1", [8, 22, 2048, 128], F32, kind="ExternalInput").ap()
    d_w2 = nc.dram_tensor("w2", [8, 16, 1408, 128], F32, kind="ExternalInput").ap()
    d_gate = nc.dram_tensor("gate", [128, 128], F32, kind="ExternalInput").ap()
    d_bq = nc.dram_tensor("bq", [128, 8], F32, kind="ExternalInput").ap()
    d_bk = nc.dram_tensor("bk", [128, 8], F32, kind="ExternalInput").ap()
    d_bv = nc.dram_tensor("bv", [128, 8], F32, kind="ExternalInput").ap()
    d_bo = nc.dram_tensor("bo", [128, 16], F32, kind="ExternalInput").ap()
    d_sel = nc.dram_tensor("sel8", [8, 1024], F32, kind="ExternalInput").ap()

    d_y = nc.dram_tensor("y", [2, 128, 16384], F32, kind="ExternalOutput").ap()

    spk = "ExternalOutput" if DEBUG else "Internal"
    d_xnsp = nc.dram_tensor("xnsp", [16, 128, 4096], BF16, kind=spk).ap()
    d_qsp = nc.dram_tensor("qsp", [8, 128, 4096], BF16, kind=spk).ap()
    d_ksp = nc.dram_tensor("ksp", [8, 128, 4096], BF16, kind=spk).ap()
    d_vsp = nc.dram_tensor("vsp", [8, 2, 128, 2048], BF16, kind=spk).ap()
    d_avsp = nc.dram_tensor("avsp", [8, 8, 128, 512], BF16, kind="Internal").ap()
    d_avag = nc.dram_tensor("avag", [8, 8192, 512], BF16, kind="Internal").ap()
    b_wq = nc.dram_tensor("b_wq", [8, 2048, 128], BF16, kind="Internal").ap()
    b_wk = nc.dram_tensor("b_wk", [8, 2048, 128], BF16, kind="Internal").ap()
    b_wv = nc.dram_tensor("b_wv", [8, 2048, 128], BF16, kind="Internal").ap()
    b_wo = nc.dram_tensor("b_wo", [16, 2048, 128], BF16, kind="Internal").ap()
    b_w1 = nc.dram_tensor("b_w1", [8, 22, 2048, 128], BF16, kind="Internal").ap()
    b_w2 = nc.dram_tensor("b_w2", [8, 16, 1408, 128], BF16, kind="Internal").ap()

    ES = ExitStack()
    Sb = ES.enter_context
    sems = {n: Sb(nc.semaphore(n)) for n in SEMS}

    ones_cf = Sb(nc.sbuf_tensor("ones_cf", [128, 1], F32))
    ones_cb = Sb(nc.sbuf_tensor("ones_cb", [128, 1], BF16))
    ones_rf = Sb(nc.sbuf_tensor("ones_rf", [1, 128], F32))
    id_bf = Sb(nc.sbuf_tensor("id_bf", [128, 128], BF16))
    id_f = Sb(nc.sbuf_tensor("id_f", [128, 128], F32))
    bq_c = Sb(nc.sbuf_tensor("bq_c", [128, 8], F32))
    bk_c = Sb(nc.sbuf_tensor("bk_c", [128, 8], F32))
    bv_c = Sb(nc.sbuf_tensor("bv_c", [128, 8], F32))
    slp_c = Sb(nc.sbuf_tensor("slp_c", [128, 8], F32))
    bo_c = Sb(nc.sbuf_tensor("bo_c", [128, 16], F32))
    gate_sb = Sb(nc.sbuf_tensor("gate_sb", [128, 128], BF16))
    sel_sb = Sb(nc.sbuf_tensor("sel_sb", [8, 1024], F32))

    counts = {r: {s: 0 for s in SEMS} for r in ROLES}
    walks = {r: Walk(nc, r, sems, counts[r]) for r in ROLES}

    def run_block(emitfn):
        with nc.Block() as blk:
            @blk.sync
            def _(eng):
                walks["sy"].bind(eng)
                emitfn(walks["sy"])

            @blk.tensor
            def _(eng):
                walks["te"].bind(eng)
                emitfn(walks["te"])

            @blk.vector
            def _(eng):
                walks["ve"].bind(eng)
                emitfn(walks["ve"])

            @blk.scalar
            def _(eng):
                walks["se"].bind(eng)
                emitfn(walks["se"])

            @blk.gpsimd
            def _(eng):
                walks["gp"].bind(eng)
                emitfn(walks["gp"])

    with ExitStack() as PA0:
        A = PA0.enter_context
        xn_sb = A(nc.sbuf_tensor("xn_sb", [128, 65536], BF16))

        with ExitStack() as PA1:
            A = PA1.enter_context
            xa = [A(nc.sbuf_tensor(f"xa{i}", [128, 2048], F32)) for i in range(2)]
            t1v = A(nc.sbuf_tensor("t1v", [128, 2048], F32))
            t1 = [t1v, t1v]
            mean_a = A(nc.sbuf_tensor("mean_a", [1, 4096], F32))
            msq_a = A(nc.sbuf_tensor("msq_a", [1, 4096], F32))
            var_a = A(nc.sbuf_tensor("var_a", [1, 4096], F32))
            rstdb = [A(nc.sbuf_tensor(f"rstdb{i}", [128, 128], F32)) for i in range(2)]
            nmrb = [A(nc.sbuf_tensor(f"nmrb{i}", [128, 128], F32)) for i in range(2)]
            pbA = A(nc.psum_tensor("pbA_a1", [128, 1024], F32))
            pbB = A(nc.psum_tensor("pbB_a1", [128, 1024], F32))
            pb4 = A(nc.psum_tensor("pb4_a1", [128, 512], F32))
            pb5 = A(nc.psum_tensor("pb5_a1", [128, 512], F32))

            def phaseA1(w: Walk):
                c = w.cnt
                w.inc(w.sy.dma_start(out=bq_c[:], in_=d_bq), "in0", 16)
                w.inc(w.sy.dma_start(out=bk_c[:], in_=d_bk), "in0", 16)
                w.inc(w.sy.dma_start(out=bv_c[:], in_=d_bv), "in0", 16)
                w.inc(w.sy.dma_start(out=bo_c[:], in_=d_bo), "in0", 16)
                w.inc(w.gp.dma_start(out=gate_sb[:], in_=d_gate), "in0", 16)
                w.inc(w.sy.dma_start(out=sel_sb[:], in_=d_sel), "in0", 16)
                w.inc(w.sy.dma_start(out=slp_c[:], in_=d_slp), "in0", 16)
                for t_src, t_dst in ((d_wq, b_wq), (d_wk, b_wk), (d_wv, b_wv)):
                    for i2 in range(0, 8, 4):
                        w.inc(w.gp.dma_start(out=t_dst[i2:i2 + 4],
                                             in_=t_src[i2:i2 + 4]), "pc", 16)
                w.gp.memset(ones_cf[:], 1.0)
                w.gp.memset(ones_cb[:], 1.0)
                w.gp.memset(ones_rf[:], 1.0)
                w.gp.memset(id_bf[:], 1.0)
                w.gp.memset(id_f[:], 1.0)
                w.gp.drain()
                w.gp.affine_select(id_bf[:], id_bf[:], pattern=[[-1, 128]],
                                       compare_op=ALU.is_equal, fill=0.0, base=0,
                                       channel_multiplier=1)
                last = w.gp.affine_select(id_f[:], id_f[:], pattern=[[-1, 128]],
                                                  compare_op=ALU.is_equal, fill=0.0, base=0,
                                                  channel_multiplier=1)
                w.inc(last, "gq", 1)

                ST = [[pb4[0:1, 0:128], pb5[0:1, 0:128]],
                      [pbA[0:1, 0:128], pbA[0:1, 512:640]]]
                stats_done = [0] * 32
                sq_done = [0] * 32
                cp_done = [0] * 32
                norm_done = [0] * 32

                def stat_copies(w, i):
                    w.wait(w.ve, "pe", stats_done[i])
                    w.ve.tensor_scalar(mean_a[0:1, i * 128:(i + 1) * 128],
                                           ST[i % 2][0], 1.0 / 2048, None, ALU.mult)
                    ins = w.ve.tensor_scalar(msq_a[0:1, i * 128:(i + 1) * 128],
                                                     ST[i % 2][1], 1.0 / 2048, None, ALU.mult)
                    cp_done[i] = w.inc(ins, "dv", 1)

                for i in range(32):
                    bf = i % 2
                    sx = "x0" if bf == 0 else "x1"
                    w.wait(w.sy, "pe", stats_done[i - 2] if i >= 2 else 0)
                    w.inc(w.sy.dma_start(
                        out=xa[bf][:].rearrange("p (ec t) -> p ec t", t=128),
                        in_=d_xt[i].rearrange("(ec p) t -> p ec t", p=128)), sx, 16)
                    ld = c[sx]

                    w.wait(w.ve, sx, ld)
                    w.wait(w.ve, "pe", stats_done[i - 1] if i >= 1 else 0)
                    for ec in range(16):
                        ins = w.ve.tensor_mul(t1[bf][:, ec * 128:(ec + 1) * 128],
                                                      xa[bf][:, ec * 128:(ec + 1) * 128],
                                                      xa[bf][:, ec * 128:(ec + 1) * 128])
                    sq_done[i] = w.inc(ins, "dv", 1)
                    if i >= 1:
                        stat_copies(w, i - 1)

                    w.wait(w.te, sx, ld)
                    w.wait(w.te, "dv", sq_done[i])
                    w.wait(w.te, "dv", cp_done[i - 2] if i >= 2 else 0)
                    for ec in range(16):
                        w.te.matmul(ST[bf][0], ones_cf[:],
                                        xa[bf][:, ec * 128:(ec + 1) * 128],
                                        start=(ec == 0), stop=(ec == 15))
                    for ec in range(16):
                        ins = w.te.matmul(ST[bf][1], ones_cf[:],
                                                  t1[bf][:, ec * 128:(ec + 1) * 128],
                                                  start=(ec == 0), stop=(ec == 15))
                    stats_done[i] = w.inc(ins, "pe", 1)
                stat_copies(w, 31)

                w.ve.drain()
                w.ve.tensor_mul(var_a[:], mean_a[:], mean_a[:])
                w.ve.drain()
                w.ve.tensor_sub(var_a[:], msq_a[:], var_a[:])
                w.ve.drain()
                ins = w.ve.tensor_scalar(var_a[:], var_a[:], 1.0e-5, None, ALU.add)
                var_done = w.inc(ins, "dv", 1)
                w.wait(w.se, "dv", var_done)
                w.se.activation(var_a[:], var_a[:], AF.Ln)
                w.se.drain()
                ins = w.se.activation(var_a[:], var_a[:], AF.Exp, scale=-0.5)
                rstd_done = w.inc(ins, "ac", 1)
                rstd_a = var_a
                w.wait(w.ve, "ac", rstd_done)
                ins = w.ve.scalar_tensor_tensor(mean_a[:], mean_a[:], -1.0, rstd_a[:],
                                                        ALU.mult, ALU.mult)
                nmr_done = w.inc(ins, "dv", 1)
                nmr_a = mean_a

                BC = [[pbB[:, 0:128], pbB[:, 512:640]],
                      [pb4[:, 0:128], pb5[:, 0:128]]]
                bc_done = [0] * 32
                for i in range(32):
                    bf = i % 2
                    sx = "x0" if bf == 0 else "x1"
                    if i < 2:
                        w.wait(w.sy, "pe", stats_done[31])
                    w.wait(w.sy, "dv", norm_done[i - 2] if i >= 2 else 0)
                    w.inc(w.sy.dma_start(
                        out=xa[bf][:].rearrange("p (ec t) -> p ec t", t=128),
                        in_=d_xt[i].rearrange("(ec p) t -> p ec t", p=128)), sx, 16)
                    ld = c[sx]

                    w.wait(w.te, "ac", rstd_done)
                    w.wait(w.te, "dv", nmr_done)
                    w.wait(w.te, "dv", norm_done[i - 2] if i >= 2 else 0)
                    w.te.matmul(BC[bf][0], ones_rf[:], rstd_a[0:1, i * 128:(i + 1) * 128],
                                    start=True, stop=True)
                    ins = w.te.matmul(BC[bf][1], ones_rf[:], nmr_a[0:1, i * 128:(i + 1) * 128],
                                          start=True, stop=True)
                    bc_done[i] = w.inc(ins, "pe", 1)

                    w.wait(w.ve, "pe", bc_done[i])
                    w.wait(w.ve, sx, ld)
                    w.ve.tensor_copy(rstdb[bf][:], BC[bf][0])
                    w.ve.tensor_copy(nmrb[bf][:], BC[bf][1])
                    w.ve.drain()
                    ins = w.ve.tensor_mul(
                        t1[bf][:].rearrange("p (ec t) -> p ec t", t=128),
                        xa[bf][:].rearrange("p (ec t) -> p ec t", t=128),
                        rstdb[bf][:].rearrange("p (o t) -> p o t", o=1).to_broadcast([128, 16, 128]))
                    w.ve.drain()
                    ins = w.ve.tensor_add(
                        xn_sb[:].rearrange("p (ec t) -> p ec t", t=4096)[:, :, i * 128:(i + 1) * 128],
                        t1[bf][:].rearrange("p (ec t) -> p ec t", t=128),
                        nmrb[bf][:].rearrange("p (o t) -> p o t", o=1).to_broadcast([128, 16, 128]))
                    norm_done[i] = w.inc(ins, "dv", 1)

                w.bar()


            if "A" in PHASES:
                run_block(phaseA1)

        with ExitStack() as PA2:
            A = PA2.enter_context
            wbufA = [A(nc.sbuf_tensor(f"wbufA{i}", [128, 2048], BF16)) for i in range(2)]
            stg = [A(nc.sbuf_tensor(f"stg{i}", [128, 2048], BF16)) for i in range(2)]
            vtr = [A(nc.sbuf_tensor(f"vtr{i}", [128, 2048], BF16)) for i in range(2)]
            pbA = A(nc.psum_tensor("pbA_a2", [128, 1024], F32))
            pbB = A(nc.psum_tensor("pbB_a2", [128, 1024], F32))
            pb4 = A(nc.psum_tensor("pb4_a2", [128, 512], F32))
            pb5 = A(nc.psum_tensor("pb5_a2", [128, 512], F32))
            trb = [A(nc.psum_tensor(f"trb{i}", [128, 1024], BF16)) for i in range(2)]

            def phaseA2(w: Walk):
                c = w.cnt
                for i2 in range(0, 16, 4):
                    w.inc(w.gp.dma_start(out=b_wo[i2:i2 + 4], in_=d_wo[i2:i2 + 4]),
                          "pc", 16)
                for ec in range(16):
                    w.inc(w.gp.dma_start(out=d_xnsp[ec],
                                         in_=xn_sb[:, ec * 4096:(ec + 1) * 4096]),
                          "sp", 16)

                PSROT = [pbA[:, 0:512], pbA[:, 512:1024], pbB[:, 0:512],
                         pbB[:, 512:1024], pb4[:, 0:512], pb5[:, 0:512]]
                st_ = {
                    "wl_free": [0, 0],
                    "stg_guard": [("s0", 0), ("s1", 0)],
                    "tr_copied": [0, 0],
                    "vtr_sp": [0, 0],
                    "rot": 0,
                    "ev_slot": [0] * 6,
                }

                def proj(dw, bias_c, n_pass, spill_fn, transpose_v, pc_need):
                    for ps in range(n_pass):
                        for oc in range(8):
                                wbf = oc % 2
                                sw = "w0" if wbf == 0 else "w1"
                                w.wait(w.sy, "pe", st_["wl_free"][wbf])
                                w.wait(w.sy, "pc", pc_need)
                                w.inc(w.sy.dma_start(
                                    out=wbufA[wbf][:].rearrange("p (ec o) -> p ec o", o=128),
                                    in_=dw[oc].rearrange("(ec p) o -> p ec o", p=128)),
                                    sw, 16)
                                wld = c[sw]
                                w.wait(w.te, sw, wld)
                                gsem, gval = st_["stg_guard"][wbf]
                                w.wait(w.ve, gsem, gval)
                                for tq in range(4):
                                    slot = st_["rot"] % 6
                                    st_["rot"] += 1
                                    t0 = (ps * 4 + tq) * 512
                                    w.wait(w.te, "dv", st_["ev_slot"][slot])
                                    for ec in range(16):
                                        ins = w.te.matmul(
                                                PSROT[slot], wbufA[wbf][:, ec * 128:(ec + 1) * 128],
                                                xn_sb[:, ec * 4096 + t0: ec * 4096 + t0 + 512],
                                                start=(ec == 0), stop=(ec == 15))
                                    mm_done = w.inc(ins, "pe", 1)
                                    w.wait(w.ve, "pe", mm_done)
                                    ins = w.ve.tensor_scalar(
                                        stg[wbf][:, tq * 512:(tq + 1) * 512], PSROT[slot],
                                        bias_c[:, oc:oc + 1], None, ALU.add)
                                    st_["ev_slot"][slot] = w.inc(ins, "dv", 1)
                                st_["wl_free"][wbf] = c["pe"]
                                ev_done = c["dv"]

                                if transpose_v:
                                    sv = "v0" if wbf == 0 else "v1"
                                    w.wait(w.te, "dv", ev_done)
                                    w.wait(w.ve, sv, st_["vtr_sp"][wbf])
                                    for j in range(16):
                                        pj = j % 2
                                        w.wait(w.te, "dv", st_["tr_copied"][pj])
                                        ins = w.te.transpose(
                                                trb[pj][:, 0:128],
                                                stg[wbf][:, j * 128:(j + 1) * 128], id_bf[:])
                                        trd = w.inc(ins, "pe", 1)
                                        w.wait(w.ve, "pe", trd)
                                        ins = w.ve.tensor_copy(
                                                vtr[wbf][:, j * 128:(j + 1) * 128],
                                                trb[pj][:, 0:128])
                                        st_["tr_copied"][pj] = w.inc(ins, "dv", 1)
                                    st_["stg_guard"][wbf] = ("pe", c["pe"])
                                    w.wait(w.gp, "dv", st_["tr_copied"][0])
                                    w.wait(w.gp, "dv", st_["tr_copied"][1])
                                    w.inc(w.gp.dma_start(
                                        out=d_vsp[oc, ps], in_=vtr[wbf][:]), sv, 16)
                                    st_["vtr_sp"][wbf] = c[sv]
                                else:
                                    ss = "s0" if wbf == 0 else "s1"
                                    w.wait(w.gp, "dv", ev_done)
                                    w.inc(w.gp.dma_start(out=spill_fn(ps, oc), in_=stg[wbf][:]),
                                          ss, 16)
                                    st_["stg_guard"][wbf] = (ss, c[ss])

                proj(b_wq, bq_c, 2,
                     lambda ps, oc: d_qsp[oc][:, ps * 2048:(ps + 1) * 2048], False, 2 * 16)
                proj(b_wk, bk_c, 2,
                     lambda ps, oc: d_ksp[oc][:, ps * 2048:(ps + 1) * 2048], False, 4 * 16)
                proj(b_wv, bv_c, 2, None, True, 6 * 16)

                w.bar()


            if "A" in PHASES:
                run_block(phaseA2)


    with ExitStack() as PB:
        A = PB.enter_context
        q_sb = A(nc.sbuf_tensor("q_sb", [128, 8192], BF16))
        b_sb = A(nc.sbuf_tensor("b_sb", [128, 16384], F32))
        k_sb = [A(nc.sbuf_tensor(f"k_sb{i}", [128, 4096], BF16)) for i in range(2)]
        v_sb = [A(nc.sbuf_tensor(f"v_sb{i}", [128, 4096], BF16)) for i in range(2)]
        ex_sb = [A(nc.sbuf_tensor(f"ex_sb{i}", [128, 1024], BF16)) for i in range(2)]
        rc_sb = A(nc.sbuf_tensor("rc_sb", [1, 512], F32))
        bc_sb = A(nc.sbuf_tensor("bc_sb", [128, 512], F32))
        avn = [A(nc.sbuf_tensor(f"avn{i}", [128, 512], BF16)) for i in range(2)]
        pbA = A(nc.psum_tensor("pbA_b", [128, 1024], F32))
        pbB = A(nc.psum_tensor("pbB_b", [128, 1024], F32))
        pb4 = A(nc.psum_tensor("pb4_b", [128, 512], F32))
        pb5 = A(nc.psum_tensor("pb5_b", [128, 512], F32))

        def phaseB(w: Walk):
            c = w.cnt
            SC = [pbA, pbB]
            kv_free = [0, 0]
            avn_sp = [0, 0]
            ex_free = [0, 0]
            sc_free = [0, 0]
            ep_dv_prev = 0
            qt_last_dv = 0
            qt_last_pe = 0
            gi = 0
            for qt in range(8):
                n_kp = 2 * (qt + 1)
                n_kt = 4 * (qt + 1)
                w.wait(w.sy, "dv", qt_last_dv)
                w.wait(w.sy, "pe", qt_last_pe)
                w.inc(w.sy.dma_start(
                    out=b_sb[:, 0:n_kt * 512].rearrange("p (kt f) -> p kt f", f=512),
                    in_=d_bt[qt, 0:n_kt].transpose([1, 0, 2])), "qb", 16)
                w.inc(w.sy.dma_start(
                    out=q_sb[:, 0:4096].rearrange("p (h t) -> p h t", t=512),
                    in_=d_qsp[:, :, qt * 512:(qt + 1) * 512].transpose([1, 0, 2])),
                    "qb", 16)
                qb_ld = c["qb"]

                for h in range(8):
                    hb = h % 2
                    skv = "kv0" if hb == 0 else "kv1"
                    w.wait(w.sy, "pe", kv_free[hb])
                    w.inc(w.sy.dma_start(out=k_sb[hb][:], in_=d_ksp[h]), skv, 16)
                    w.inc(w.sy.dma_start(out=v_sb[hb][:, 0:2048], in_=d_vsp[h, 0]),
                          skv, 16)
                    w.inc(w.sy.dma_start(out=v_sb[hb][:, 2048:4096], in_=d_vsp[h, 1]),
                          skv, 16)
                    kv_ld = c[skv]

                    w.wait(w.te, "qb", qb_ld)
                    w.wait(w.te, skv, kv_ld)
                    sc_done = [0] * n_kp
                    exp_done = [0] * n_kp
                    for kp in range(n_kp + 1):
                        pp = kp % 2
                        if kp < n_kp:
                            w.wait(w.te, "ac", sc_free[pp])
                            for u in range(2):
                                kt = 2 * kp + u
                                ins = w.te.matmul(
                                    SC[pp][:, u * 512:(u + 1) * 512],
                                    k_sb[hb][:, kt * 128:(kt + 1) * 128],
                                    q_sb[:, h * 512:(h + 1) * 512],
                                    start=True, stop=True)
                            sc_done[kp] = w.inc(ins, "pe", 1)
                        if kp >= 1:
                            km = kp - 1
                            pm = km % 2
                            w.wait(w.te, "ac", exp_done[km])
                            if km == 0:
                                w.wait(w.te, "dv", ep_dv_prev)
                            for u in range(2):
                                kt = 2 * km + u
                                w.te.matmul(pb5[0:1, 0:512], ones_cb[:],
                                            ex_sb[pm][:, u * 512:(u + 1) * 512],
                                            start=(kt == 0), stop=(kt == n_kt - 1))
                                ins = w.te.matmul(pb4[:, 0:512],
                                                  v_sb[hb][:, kt * 128:(kt + 1) * 128],
                                                  ex_sb[pm][:, u * 512:(u + 1) * 512],
                                                  start=(kt == 0), stop=(kt == n_kt - 1))
                            ex_free[pm] = w.inc(ins, "pe", 1)
                        if kp < n_kp:
                            w.wait(w.ve, "pe", sc_done[kp])
                            w.wait(w.ve, "qb", qb_ld)
                            ins = w.ve.scalar_tensor_tensor(
                                SC[pp][:], b_sb[:, kp * 1024:(kp + 1) * 1024],
                                slp_c[:, h:h + 1],
                                SC[pp][:], ALU.mult, ALU.add)
                            bias_done = w.inc(ins, "dv", 1)
                            w.wait(w.se, "dv", bias_done)
                            w.wait(w.se, "pe", ex_free[pp])
                            ins = w.se.activation(ex_sb[pp][:], SC[pp][:], AF.Exp,
                                                  scale=1.0 / RD)
                            exp_done[kp] = w.inc(ins, "ac", 1)
                            sc_free[pp] = exp_done[kp]
                    kv_free[hb] = c["pe"]

                    w.wait(w.ve, "pe", c["pe"])
                    ins = w.ve.reciprocal(rc_sb[:], pb5[0:1, 0:512])
                    rc_done = w.inc(ins, "dv", 1)
                    w.wait(w.te, "dv", rc_done)
                    ins = w.te.matmul(pb5[:, 0:512], ones_rf[:], rc_sb[:],
                                      start=True, stop=True)
                    bc_done = w.inc(ins, "pe", 1)
                    w.wait(w.ve, "pe", bc_done)
                    w.ve.tensor_copy(bc_sb[:], pb5[:, 0:512])
                    w.ve.drain()
                    sa = "a0" if hb == 0 else "a1"
                    w.wait(w.ve, sa, avn_sp[hb])
                    ins = w.ve.tensor_mul(avn[hb][:], pb4[:, 0:512], bc_sb[:])
                    avn_done = w.inc(ins, "dv", 1)
                    ep_dv_prev = avn_done
                    w.wait(w.gp, "dv", avn_done)
                    w.inc(w.gp.dma_start(out=d_avsp[qt, h], in_=avn[hb][:]), sa, 16)
                    avn_sp[hb] = c[sa]
                    if gi < 8:
                        w.inc(w.gp.dma_start(out=b_w1[gi], in_=d_w1[gi]), "pc", 16)
                    elif gi < 16:
                        w.inc(w.gp.dma_start(out=b_w2[gi - 8], in_=d_w2[gi - 8]),
                              "pc", 16)
                    gi += 1
                qt_last_dv = c["dv"]
                qt_last_pe = c["pe"]
                w.wait(w.gp, "a0", c["a0"])
                w.wait(w.gp, "a1", c["a1"])
                ins = w.gp.collective_compute(
                    "AllGather", ALU.bypass,
                    replica_groups=[[0, 1, 2, 3, 4, 5, 6, 7]],
                    ins=[d_avsp[qt].rearrange("h p t -> (h p) t")],
                    outs=[d_avag[qt]])
                w.inc(ins, "ag", 1)
            w.bar()

        if "B" in PHASES:
            run_block(phaseB)

    with ExitStack() as PC:
        A = PC.enter_context
        xn_c = A(nc.sbuf_tensor("xn_c", [128, 16384], BF16))
        av_c = A(nc.sbuf_tensor("av_c", [128, 16384], BF16))
        acc = A(nc.sbuf_tensor("acc", [128, 16384], F32))
        act_sb = A(nc.sbuf_tensor("act_sb", [128, 11264], BF16))
        wbufC = [A(nc.sbuf_tensor(f"wbufC{i}", [128, 2048], BF16)) for i in range(2)]
        wubuf = [A(nc.sbuf_tensor(f"wubuf{i}", [128, 2048], BF16)) for i in range(2)]
        sg_sb = [A(nc.sbuf_tensor(f"sg_sb{i}", [128, 512], F32)) for i in range(2)]
        cb_sb = A(nc.sbuf_tensor("cb_sb", [128, 1024], F32))
        tmp_sb = [A(nc.sbuf_tensor(f"tmp_sb{i}", [128, 512], F32)) for i in range(2)]
        xr_sb = [A(nc.sbuf_tensor(f"xr_sb{i}", [128, 2048], F32)) for i in range(2)]
        cmbT = A(nc.sbuf_tensor("cmbT", [8, 1024], F32))
        lgt = A(nc.sbuf_tensor("lgt", [128, 8], F32))
        negm = A(nc.sbuf_tensor("negm", [128, 1], F32))
        egl = A(nc.sbuf_tensor("egl", [128, 8], F32))
        ssum = A(nc.sbuf_tensor("ssum", [128, 1], F32))
        rsu = A(nc.sbuf_tensor("rsu", [128, 1], F32))
        p8 = A(nc.sbuf_tensor("p8", [128, 8], F32))
        gemask = A(nc.sbuf_tensor("gemask", [128, 8], F32))
        m12 = A(nc.sbuf_tensor("m12", [128, 1], F32))
        cmb = A(nc.sbuf_tensor("cmb", [128, 8], F32))
        pbA = A(nc.psum_tensor("pbA_c", [128, 1024], F32))
        pbB = A(nc.psum_tensor("pbB_c", [128, 1024], F32))
        pb4 = A(nc.psum_tensor("pb4_c", [128, 512], F32))
        pb5 = A(nc.psum_tensor("pb5_c", [128, 512], F32))
        pb6 = A(nc.psum_tensor("pb6_c", [128, 512], F32))
        pb7 = A(nc.psum_tensor("pb7_c", [128, 512], F32))

        def phaseC(w: Walk):
            c = w.cnt
            s = {
                "wl_free": [0, 0],
                "wu_free": [0, 0],
                "wo_ev": [0, 0],
                "wo_last": 0,
                "o_used": [0, 0],
                "o_last": 0,
                "xr_used": [0, 0],
                "gate_dv": 0,
                "sg_done": 0,
                "act_done": 0,
                "sg_par": [0, 0],
                "act_par": [0, 0],
                "cb_used": 0,
                "w2_all_prev": 0,
                "st_prev": 0,
                "th_pe_prev": 0,
            }
            d_avag4 = d_avag.rearrange("q (dc p) t -> q dc p t", p=128)
            for th in range(2):
                t0 = th * 1024
                w.wait(w.sy, "pe", s["th_pe_prev"])
                w.sw_dma_sy("cl", lambda eng, lg, t0=t0: eng.dma_start(
                    out=xn_c[:].rearrange("p (ec t) -> p ec t", t=1024),
                    in_=d_xnsp[:, :, (lg % 2) * 2048 + t0: (lg % 2) * 2048 + t0 + 1024]
                        .transpose([1, 0, 2])))
                for qq in range(2):
                    w.sw_dma_sy("cl", lambda eng, lg, th=th, qq=qq: eng.dma_start(
                        out=av_c[:].rearrange("p (ec t) -> p ec t", t=1024)
                            [:, :, qq * 512:(qq + 1) * 512],
                        in_=d_avag4[(lg % 2) * 4 + th * 2 + qq,
                                    (lg % 8 // 2) * 16:(lg % 8 // 2) * 16 + 16]
                            .transpose([1, 0, 2])))
                cl_ld = c["cl"]

                PSWP = [[pbA[:, 0:512], pbA[:, 512:1024]],
                        [pbB[:, 0:512], pbB[:, 512:1024]]]
                for oc in (range(16) if "W" in CPARTS else []):
                    wbf = oc % 2
                    sw = "w0" if wbf == 0 else "w1"
                    w.wait(w.sy, "pe", s["wl_free"][wbf])
                    w.wait(w.sy, "pc", 26 * 16)
                    w.inc(w.sy.dma_start(
                        out=wbufC[wbf][:].rearrange("p (ec o) -> p ec o", o=128),
                        in_=b_wo[oc].rearrange("(ec p) o -> p ec o", p=128)), sw, 16)
                    wld = c[sw]
                    PSW = PSWP[wbf]
                    w.wait(w.te, sw, wld)
                    w.wait(w.te, "cl", cl_ld)
                    w.wait(w.te, "dv", s["wo_ev"][wbf])
                    w.wait(w.te, "ac", s["sg_done"])
                    w.wait(w.te, "dv", s["act_done"])
                    for ec in range(16):
                        for tq in range(2):
                            ins = w.te.matmul(
                                PSW[tq], wbufC[wbf][:, ec * 128:(ec + 1) * 128],
                                av_c[:, ec * 1024 + tq * 512: ec * 1024 + tq * 512 + 512],
                                start=(ec == 0), stop=(ec == 15))
                    mm_done = w.inc(ins, "pe", 1)
                    s["wl_free"][wbf] = c["pe"]
                    w.wait(w.ve, "pe", mm_done)
                    w.wait(w.ve, "st", s["st_prev"])
                    for tq in range(2):
                        ins = w.ve.tensor_scalar(
                            acc[:, oc * 1024 + tq * 512: oc * 1024 + tq * 512 + 512],
                            PSW[tq], bo_c[:, oc:oc + 1], None, ALU.add)
                    s["wo_ev"][wbf] = w.inc(ins, "dv", 1)
                    s["wo_last"] = s["wo_ev"][wbf]

                for tc in (range(8) if "G" in CPARTS else []):
                    w.wait(w.te, "in0", 112)
                    w.wait(w.te, "dv", s["gate_dv"])
                    w.wait(w.te, "dv", s["o_last"])
                    w.wait(w.te, "dv", s["act_done"])
                    for ec in range(16):
                        ins = w.te.matmul(
                            pb5[:, 0:8],
                            xn_c[:, ec * 1024 + tc * 128: ec * 1024 + (tc + 1) * 128],
                            gate_sb[:, ec * 8:(ec + 1) * 8],
                            start=(ec == 0), stop=(ec == 15))
                    lg_done = w.inc(ins, "pe", 1)

                    w.wait(w.ve, "pe", lg_done)
                    w.ve.tensor_copy(lgt[:], pb5[:, 0:8])
                    w.ve.drain()
                    ins = w.ve.reduce_max(negm[:], lgt[:], axis=AX.X, negate=True)
                    nm_done = w.inc(ins, "dv", 1)
                    w.wait(w.se, "dv", nm_done)
                    ins = w.se.activation(egl[:], lgt[:], AF.Exp, bias=negm[:])
                    eg_done = w.inc(ins, "ac", 1)
                    w.wait(w.ve, "ac", eg_done)
                    w.ve.reduce_sum(ssum[:], egl[:], axis=AX.X)
                    w.ve.drain()
                    w.ve.reciprocal(rsu[:], ssum[:])
                    w.ve.drain()
                    w.ve.tensor_scalar(cmb[:], egl[:], rsu[:], None, ALU.mult)
                    w.ve.drain()
                    w.ve.max(out=p8[:], in_=cmb[:])
                    w.ve.drain()
                    w.ve.tensor_scalar(gemask[:], cmb[:], p8[:, 1:2], None, ALU.is_ge)
                    w.ve.tensor_add(m12[:], p8[:, 0:1], p8[:, 1:2])
                    w.ve.drain()
                    w.ve.reciprocal(m12[:], m12[:])
                    w.ve.drain()
                    w.ve.tensor_scalar(cmb[:], cmb[:], m12[:], None, ALU.mult)
                    w.ve.drain()
                    ins = w.ve.tensor_mul(cmb[:], cmb[:], gemask[:])
                    cmb_done = w.inc(ins, "dv", 1)

                    w.wait(w.te, "dv", cmb_done)
                    w.wait(w.te, "gq", 1)
                    ins = w.te.transpose(pb4[0:8, 0:128], cmb[:], id_f[:])
                    tr_done = w.inc(ins, "pe", 1)
                    w.wait(w.ve, "pe", tr_done)
                    ins = w.ve.tensor_copy(cmbT[:, tc * 128:(tc + 1) * 128],
                                           pb4[0:8, 0:128])
                    s["gate_dv"] = w.inc(ins, "dv", 1)

                PSGP = [[pbB[:, 0:512], pbB[:, 512:1024]], [pb4[:, 0:512], pb5[:, 0:512]]]
                PSUP = [[pbA[:, 0:512], pbA[:, 512:1024]], [pb6[:, 0:512], pb7[:, 0:512]]]
                PSOP = [[pb4[:, 0:512], pb5[:, 0:512]], [pb6[:, 0:512], pb7[:, 0:512]]]
                for e in (range(EXP_N) if "E" in CPARTS else []):
                    w.wait(w.te, "dv", s["gate_dv"])
                    w.wait(w.te, "dv", s["cb_used"])
                    w.wait(w.te, "dv", s["wo_last"])
                    w.wait(w.te, "ac", s["sg_done"])
                    w.wait(w.te, "dv", s["act_done"])
                    for ts in range(2):
                        if ts == 1:
                            w.wait(w.te, "dv", c["dv"])
                        ins = w.te.matmul(pb6[:, 0:512],
                                          sel_sb[:, e * 128:(e + 1) * 128],
                                          cmbT[:, ts * 512:(ts + 1) * 512],
                                          start=True, stop=True)
                        cbb = w.inc(ins, "pe", 1)
                        w.wait(w.ve, "pe", cbb)
                        ins = w.ve.tensor_copy(cb_sb[:, ts * 512:(ts + 1) * 512],
                                               pb6[:, 0:512])
                        w.inc(ins, "dv", 1)
                    cb_ready = c["dv"]

                    for j in (range(11) if EXP_STAGE in ("w1", "full") else []):
                        jb = j % 2
                        swg = "w0" if jb == 0 else "w1"
                        swu = "w2" if jb == 0 else "w3"
                        w.wait(w.sy, "pe", s["wl_free"][jb])
                        w.inc(w.sy.dma_start(
                            out=wbufC[jb][:].rearrange("p (ec o) -> p ec o", o=128),
                            in_=b_w1[e, j].rearrange("(ec p) o -> p ec o", p=128)),
                            swg, 16)
                        w.wait(w.sy, "pe", s["wu_free"][jb])
                        w.inc(w.sy.dma_start(
                            out=wubuf[jb][:].rearrange("p (ec o) -> p ec o", o=128),
                            in_=b_w1[e, j + 11].rearrange("(ec p) o -> p ec o", p=128)),
                            swu, 16)
                        wldg, wldu = c[swg], c[swu]

                        jp = j % 2
                        PSG, PSU = PSGP[jp], PSUP[jp]
                        w.wait(w.te, swg, wldg)
                        w.wait(w.te, swu, wldu)
                        w.wait(w.te, "ac", s["sg_par"][jp])
                        w.wait(w.te, "dv", s["act_par"][jp])
                        w.wait(w.te, "dv", cb_ready)
                        for ts in range(2):
                            for ec in range(16):
                                w.te.matmul(
                                    PSG[ts], wbufC[jb][:, ec * 128:(ec + 1) * 128],
                                    xn_c[:, ec * 1024 + ts * 512: ec * 1024 + ts * 512 + 512],
                                    start=(ec == 0), stop=(ec == 15))
                        for ts in range(2):
                            for ec in range(16):
                                ins = w.te.matmul(
                                    PSU[ts], wubuf[jb][:, ec * 128:(ec + 1) * 128],
                                    xn_c[:, ec * 1024 + ts * 512: ec * 1024 + ts * 512 + 512],
                                    start=(ec == 0), stop=(ec == 15))
                        w1_done = w.inc(ins, "pe", 1)
                        s["wl_free"][jb] = c["pe"]
                        s["wu_free"][jb] = c["pe"]

                        w.wait(w.se, "pe", w1_done)
                        w.wait(w.se, "dv", s["act_done"])
                        for ts in range(2):
                            ins = w.se.activation(sg_sb[ts][:], PSG[ts], AF.Silu)
                        s["sg_done"] = w.inc(ins, "ac", 1)
                        s["sg_par"][jp] = s["sg_done"]
                        w.wait(w.ve, "ac", s["sg_done"])
                        w.wait(w.ve, "pe", s["w2_all_prev"])
                        for ts in range(2):
                            ins = w.ve.tensor_mul(
                                act_sb[:, j * 1024 + ts * 512: j * 1024 + ts * 512 + 512],
                                sg_sb[ts][:], PSU[ts])
                        s["act_done"] = w.inc(ins, "dv", 1)
                        s["act_par"][jp] = s["act_done"]

                    for hc in (range(16) if EXP_STAGE == "full" else []):
                        hb2 = hc % 2
                        sw = "w0" if hb2 == 0 else "w1"
                        w.wait(w.sy, "pe", s["wl_free"][hb2])
                        w.inc(w.sy.dma_start(
                            out=wbufC[hb2][:, 0:1408].rearrange("p (j o) -> p j o", o=128),
                            in_=b_w2[e, hc].rearrange("(j p) o -> p j o", p=128)),
                            sw, 16)
                        wld = c[sw]
                        PSO = PSOP[hb2]
                        w.wait(w.te, sw, wld)
                        w.wait(w.te, "dv", s["act_done"])
                        w.wait(w.te, "ac", s["sg_done"])
                        w.wait(w.te, "dv", s["o_used"][hb2])
                        for ts in range(2):
                            for j in range(11):
                                ins = w.te.matmul(
                                    PSO[ts], wbufC[hb2][:, j * 128:(j + 1) * 128],
                                    act_sb[:, j * 1024 + ts * 512: j * 1024 + ts * 512 + 512],
                                    start=(j == 0), stop=(j == 10))
                        w2_done = w.inc(ins, "pe", 1)
                        s["wl_free"][hb2] = c["pe"]

                        w.wait(w.ve, "pe", w2_done)
                        w.wait(w.ve, "dv", cb_ready)
                        for ts in range(2):
                            w.ve.tensor_mul(tmp_sb[ts][:], PSO[ts],
                                            cb_sb[:, ts * 512:(ts + 1) * 512])
                        w.ve.drain()
                        for ts in range(2):
                            a0 = hc * 1024 + ts * 512
                            ins = w.ve.tensor_add(acc[:, a0:a0 + 512],
                                                  acc[:, a0:a0 + 512], tmp_sb[ts][:])
                        s["o_used"][hb2] = w.inc(ins, "dv", 1)
                        s["o_last"] = s["o_used"][hb2]
                        w.ve.drain()
                    s["cb_used"] = c["dv"]
                    s["w2_all_prev"] = c["pe"]

                for qq in (range(8) if "R" in CPARTS else []):
                    qb2 = qq % 2
                    sx = "xr0" if qb2 == 0 else "xr1"
                    w.wait(w.sy, "dv", s["xr_used"][qb2])
                    w.inc(w.sy.dma_start(
                        out=xr_sb[qb2][:].rearrange("p (ec t) -> p ec t", t=128),
                        in_=d_xo[8 * th + qq].rearrange("(ec p) t -> p ec t", p=128)),
                        sx, 16)
                    xld = c[sx]
                    w.wait(w.ve, sx, xld)
                    if qq == 0:
                        w.ve.drain()
                    acc3 = acc[:].rearrange("p (oc t) -> p oc t", t=1024)
                    xr3 = xr_sb[qb2][:].rearrange("p (ec t) -> p ec t", t=128)
                    ins = w.ve.tensor_add(
                        acc3[:, :, qq * 128:(qq + 1) * 128],
                        acc3[:, :, qq * 128:(qq + 1) * 128], xr3)
                    s["xr_used"][qb2] = w.inc(ins, "dv", 1)
                    w.ve.drain()
                res_done = c["dv"]
                w.wait(w.sy, "dv", res_done)
                w.inc(w.sy.dma_start(out=d_y[th], in_=acc[:]), "st", 16)
                s["st_prev"] = c["st"]
                s["th_pe_prev"] = c["pe"]

            w.wait(w.sy, "st", c["st"])
            w.bar()

        if "C" in PHASES:
            run_block(phaseC)

    ES.close()
    return nc



def _pack_shared(inp):
    f32 = np.float32
    Wq = np.asarray(inp["L0_attn_Wq"], f32)
    Wk = np.asarray(inp["L0_attn_Wk"], f32)
    Wv = np.asarray(inp["L0_attn_Wv"], f32)
    Wo = np.asarray(inp["L0_attn_Wo"], f32)
    w1 = np.asarray(inp["L0_ffn_w1"], f32)
    w2 = np.asarray(inp["L0_ffn_w2"], f32)
    gw = np.asarray(inp["L0_ffn_gate_w"], f32)

    def blkT(W):
        return np.ascontiguousarray(
            W.T.reshape(E, W.shape[0] // 128, 128).transpose(1, 0, 2))

    qg = np.arange(4096)
    kg = np.arange(4096)
    M = np.where(kg[:, None] <= qg[None, :],
                 (kg[:, None] - qg[None, :]).astype(f32), f32(NEG))
    btab = np.ascontiguousarray(M.reshape(32, 128, 8, 512).transpose(2, 0, 1, 3))

    return {
        "_wqf": blkT(Wq), "_wkf": blkT(Wk), "_wvf": blkT(Wv), "wo": blkT(Wo),
        "btab": btab,
        "w1": np.ascontiguousarray(
            w1.transpose(0, 2, 1).reshape(8, E, 22, 128).transpose(0, 2, 1, 3)),
        "w2": np.ascontiguousarray(
            w2.transpose(0, 2, 1).reshape(8, NI, 16, 128).transpose(0, 2, 1, 3)),
        "gate": np.ascontiguousarray(
            gw.T.reshape(16, 128, 8).transpose(1, 0, 2).reshape(128, 128)),
        "_bqf": np.ascontiguousarray(np.asarray(inp["L0_attn_bq"], f32).reshape(16, 128).T),
        "_bkf": np.ascontiguousarray(np.asarray(inp["L0_attn_bk"], f32).reshape(16, 128).T),
        "_bvf": np.ascontiguousarray(np.asarray(inp["L0_attn_bv"], f32).reshape(16, 128).T),
        "bo": np.ascontiguousarray(np.asarray(inp["L0_attn_bo"], f32).reshape(16, 128).T),
        "sel8": np.ascontiguousarray(np.repeat(np.eye(8, dtype=f32), 128, axis=1)),
    }


def _pack_core(inp, core, shared):
    f32 = np.float32
    b, hg = core // 2, core % 2
    x = np.asarray(inp["L0_attn_x"], f32)[b]
    xT = np.ascontiguousarray(x.T)
    xt = np.ascontiguousarray(xT.reshape(E, 32, 128).transpose(1, 0, 2))
    hs = slice(hg * 8, (hg + 1) * 8)
    slp = np.broadcast_to(
        np.asarray([SLOPES[hg * 8 + i] * RD for i in range(8)], f32)[None, :],
        (128, 8))
    d = {
        "xt": xt,
        "xo": np.ascontiguousarray(xt[hg * 16:(hg + 1) * 16]),
        "wq": np.ascontiguousarray(shared["_wqf"][hs]),
        "wk": np.ascontiguousarray(shared["_wkf"][hs]),
        "wv": np.ascontiguousarray(shared["_wvf"][hs]),
        "bq": np.ascontiguousarray(shared["_bqf"][:, hs]),
        "bk": np.ascontiguousarray(shared["_bkf"][:, hs]),
        "bv": np.ascontiguousarray(shared["_bvf"][:, hs]),
        "slp": np.ascontiguousarray(slp),
    }
    for k2, v2 in shared.items():
        if not k2.startswith("_"):
            d[k2] = v2
    return d


def kernel(**inputs):
    nc = build_program()
    shared = _pack_shared(inputs)
    in_maps = [_pack_core(inputs, cc, shared) for cc in range(8)]
    res = run_bass_kernel_spmd(nc, in_maps, core_ids=list(range(8)))
    out = np.zeros((4, 4096, 2048), dtype=np.float32)
    for cc in range(8):
        b, h = cc // 2, cc % 2
        y = np.asarray(res.results[cc]["y"])
        blk = y.reshape(2, 128, 16, 1024).transpose(0, 3, 2, 1).reshape(2048, 2048)
        out[b, h * 2048:(h + 1) * 2048, :] = blk
    return out


# version 09 (final)
# speedup vs baseline: 1.0347x; 1.0347x over previous
import math
import numpy as np
from contextlib import ExitStack

import concourse.bass as bass
import concourse.mybir as mybir
from concourse.bass_utils import run_bass_kernel_spmd
import concourse.bass_utils as _bu

LDW_OPT = True
if not hasattr(_bu, "_orig_run_command"):
    _bu._orig_run_command = _bu.run_command

    def _patched_run_command(cmd, *a, **kw):
        if LDW_OPT and isinstance(cmd, list):
            cmd = ["--enable-ldw-opt=true" if c == "--enable-ldw-opt=false" else c
                   for c in cmd]
        return _bu._orig_run_command(cmd, *a, **kw)

    _bu.run_command = _patched_run_command

F32 = mybir.dt.float32
BF16 = mybir.dt.bfloat16
AF = mybir.ActivationFunctionType
ALU = mybir.AluOpType
AX = mybir.AxisListType

E = 2048
NI = 1408
NEG = -1.0e30
RD = math.sqrt(128.0)
SLOPES = [2.0 ** (-0.5 * (i + 1)) for i in range(16)]

DEBUG = False
PHASES = "ABC"
CPARTS = "WGER"
EXP_N = 8
EXP_STAGE = "full"
SW_MODE = "switch"
AG_MODE = "cc"

ROLES = ("sy", "te", "ve", "se", "gp")

SEMS = ["pe", "dv", "ac", "gq", "sp", "st", "pc", "ag", "x0", "x1", "w0", "w1", "w2", "w3",
        "kv0", "kv1", "qb", "cl", "xr0", "xr1", "in0",
        "s0", "s1", "v0", "v1", "a0", "a1"]


class Proxy:
    def __init__(self, real):
        self._r = real

    def __getattr__(self, name):
        if self._r is None:
            return lambda *a, **k: None
        return getattr(self._r, name)


class Walk:

    def __init__(self, nc, role, sems, cnt):
        self.nc = nc
        self.role = role
        self.sems = sems
        self.cnt = cnt

    def bind(self, eng):
        self.sy = Proxy(eng if self.role == "sy" else None)
        self.te = Proxy(eng if self.role == "te" else None)
        self.ve = Proxy(eng if self.role == "ve" else None)
        self.se = Proxy(eng if self.role == "se" else None)
        self.gp = Proxy(eng if self.role == "gp" else None)

    def inc(self, inst, sem, k=1):
        self.cnt[sem] += k
        if inst is not None:
            inst.then_inc(self.sems[sem], k)
        return self.cnt[sem]

    def wait(self, proxy, sem, val):
        if val > 0:
            proxy.wait_ge(self.sems[sem], val)

    def sw_dma_sy(self, sem, build):
        if self.sy._r is not None:
            eng = self.sy._r
            if SW_MODE == "switch":
                for arm in eng.switch_core_id(n=128):
                    try:
                        lg = int(arm.logical)
                    except Exception:
                        lg = int(arm)
                    build(eng, lg).then_inc(self.sems[sem], 16)
            else:
                build(eng, 0).then_inc(self.sems[sem], 16)
        self.cnt[sem] += 16

    def bar(self, skip=()):
        for s in SEMS:
            if s in skip:
                continue
            if self.cnt[s] > 0:
                for p in (self.sy, self.te, self.ve, self.se, self.gp):
                    if p._r is not None:
                        p.wait_ge(self.sems[s], self.cnt[s])


def build_program():
    nc = bass.Bass("TRN2", target_bir_lowering=False)

    d_xt = nc.dram_tensor("xt", [32, 2048, 128], F32, kind="ExternalInput").ap()
    d_bt = nc.dram_tensor("btab", [8, 32, 128, 512], F32, kind="ExternalInput").ap()
    d_wq = nc.dram_tensor("wq", [8, 2048, 128], F32, kind="ExternalInput").ap()
    d_wk = nc.dram_tensor("wk", [8, 2048, 128], F32, kind="ExternalInput").ap()
    d_wv = nc.dram_tensor("wv", [8, 2048, 128], F32, kind="ExternalInput").ap()
    d_xo = nc.dram_tensor("xo", [16, 2048, 128], F32, kind="ExternalInput").ap()
    d_slp = nc.dram_tensor("slp", [128, 8], F32, kind="ExternalInput").ap()
    d_wo = nc.dram_tensor("wo", [16, 2048, 128], F32, kind="ExternalInput").ap()
    d_w1 = nc.dram_tensor("w1", [8, 22, 2048, 128], F32, kind="ExternalInput").ap()
    d_w2 = nc.dram_tensor("w2", [8, 16, 1408, 128], F32, kind="ExternalInput").ap()
    d_gate = nc.dram_tensor("gate", [128, 128], F32, kind="ExternalInput").ap()
    d_bq = nc.dram_tensor("bq", [128, 8], F32, kind="ExternalInput").ap()
    d_bk = nc.dram_tensor("bk", [128, 8], F32, kind="ExternalInput").ap()
    d_bv = nc.dram_tensor("bv", [128, 8], F32, kind="ExternalInput").ap()
    d_bo = nc.dram_tensor("bo", [128, 16], F32, kind="ExternalInput").ap()
    d_sel = nc.dram_tensor("sel8", [8, 1024], F32, kind="ExternalInput").ap()

    d_y = nc.dram_tensor("y", [2, 128, 16384], F32, kind="ExternalOutput").ap()

    spk = "ExternalOutput" if DEBUG else "Internal"
    d_xnsp = nc.dram_tensor("xnsp", [16, 128, 4096], BF16, kind=spk).ap()
    d_qsp = nc.dram_tensor("qsp", [8, 128, 4096], BF16, kind=spk).ap()
    d_ksp = nc.dram_tensor("ksp", [8, 128, 4096], BF16, kind=spk).ap()
    d_vsp = nc.dram_tensor("vsp", [8, 2, 128, 2048], BF16, kind=spk).ap()
    d_avsp = nc.dram_tensor("avsp", [8, 8, 128, 512], BF16, kind="Internal").ap()
    d_avag = nc.dram_tensor("avag", [8, 8192, 512], BF16, kind="Internal").ap()
    b_wq = nc.dram_tensor("b_wq", [8, 2048, 128], BF16, kind="Internal").ap()
    b_wk = nc.dram_tensor("b_wk", [8, 2048, 128], BF16, kind="Internal").ap()
    b_wv = nc.dram_tensor("b_wv", [8, 2048, 128], BF16, kind="Internal").ap()
    b_wo = nc.dram_tensor("b_wo", [16, 2048, 128], BF16, kind="Internal").ap()
    b_w1 = nc.dram_tensor("b_w1", [8, 22, 2048, 128], BF16, kind="Internal").ap()
    b_w2 = nc.dram_tensor("b_w2", [8, 16, 1408, 128], BF16, kind="Internal").ap()

    ES = ExitStack()
    Sb = ES.enter_context
    sems = {n: Sb(nc.semaphore(n)) for n in SEMS}

    ones_cf = Sb(nc.sbuf_tensor("ones_cf", [128, 1], F32))
    ones_cb = Sb(nc.sbuf_tensor("ones_cb", [128, 1], BF16))
    ones_rf = Sb(nc.sbuf_tensor("ones_rf", [1, 128], F32))
    id_bf = Sb(nc.sbuf_tensor("id_bf", [128, 128], BF16))
    id_f = Sb(nc.sbuf_tensor("id_f", [128, 128], F32))
    bq_c = Sb(nc.sbuf_tensor("bq_c", [128, 8], F32))
    bk_c = Sb(nc.sbuf_tensor("bk_c", [128, 8], F32))
    bv_c = Sb(nc.sbuf_tensor("bv_c", [128, 8], F32))
    slp_c = Sb(nc.sbuf_tensor("slp_c", [128, 8], F32))
    bo_c = Sb(nc.sbuf_tensor("bo_c", [128, 16], F32))
    gate_sb = Sb(nc.sbuf_tensor("gate_sb", [128, 128], BF16))
    sel_sb = Sb(nc.sbuf_tensor("sel_sb", [8, 1024], F32))

    counts = {r: {s: 0 for s in SEMS} for r in ROLES}
    walks = {r: Walk(nc, r, sems, counts[r]) for r in ROLES}

    def run_block(emitfn):
        with nc.Block() as blk:
            @blk.sync
            def _(eng):
                walks["sy"].bind(eng)
                emitfn(walks["sy"])

            @blk.tensor
            def _(eng):
                walks["te"].bind(eng)
                emitfn(walks["te"])

            @blk.vector
            def _(eng):
                walks["ve"].bind(eng)
                emitfn(walks["ve"])

            @blk.scalar
            def _(eng):
                walks["se"].bind(eng)
                emitfn(walks["se"])

            @blk.gpsimd
            def _(eng):
                walks["gp"].bind(eng)
                emitfn(walks["gp"])

    with ExitStack() as PA0:
        A = PA0.enter_context
        xn_sb = A(nc.sbuf_tensor("xn_sb", [128, 65536], BF16))

        with ExitStack() as PA1:
            A = PA1.enter_context
            xa = [A(nc.sbuf_tensor(f"xa{i}", [128, 2048], F32)) for i in range(2)]
            t1v = A(nc.sbuf_tensor("t1v", [128, 2048], F32))
            t1 = [t1v, t1v]
            mean_a = A(nc.sbuf_tensor("mean_a", [1, 4096], F32))
            msq_a = A(nc.sbuf_tensor("msq_a", [1, 4096], F32))
            var_a = A(nc.sbuf_tensor("var_a", [1, 4096], F32))
            rstdb = [A(nc.sbuf_tensor(f"rstdb{i}", [128, 128], F32)) for i in range(2)]
            nmrb = [A(nc.sbuf_tensor(f"nmrb{i}", [128, 128], F32)) for i in range(2)]
            pbA = A(nc.psum_tensor("pbA_a1", [128, 1024], F32))
            pbB = A(nc.psum_tensor("pbB_a1", [128, 1024], F32))
            pb4 = A(nc.psum_tensor("pb4_a1", [128, 512], F32))
            pb5 = A(nc.psum_tensor("pb5_a1", [128, 512], F32))

            def phaseA1(w: Walk):
                c = w.cnt
                w.inc(w.sy.dma_start(out=bq_c[:], in_=d_bq), "in0", 16)
                w.inc(w.sy.dma_start(out=bk_c[:], in_=d_bk), "in0", 16)
                w.inc(w.sy.dma_start(out=bv_c[:], in_=d_bv), "in0", 16)
                w.inc(w.sy.dma_start(out=bo_c[:], in_=d_bo), "in0", 16)
                w.inc(w.gp.dma_start(out=gate_sb[:], in_=d_gate), "in0", 16)
                w.inc(w.sy.dma_start(out=sel_sb[:], in_=d_sel), "in0", 16)
                w.inc(w.sy.dma_start(out=slp_c[:], in_=d_slp), "in0", 16)
                for t_src, t_dst in ((d_wq, b_wq), (d_wk, b_wk), (d_wv, b_wv)):
                    for i2 in range(0, 8, 4):
                        w.inc(w.gp.dma_start(out=t_dst[i2:i2 + 4],
                                             in_=t_src[i2:i2 + 4]), "pc", 16)
                w.gp.memset(ones_cf[:], 1.0)
                w.gp.memset(ones_cb[:], 1.0)
                w.gp.memset(ones_rf[:], 1.0)
                w.gp.memset(id_bf[:], 1.0)
                w.gp.memset(id_f[:], 1.0)
                w.gp.drain()
                w.gp.affine_select(id_bf[:], id_bf[:], pattern=[[-1, 128]],
                                       compare_op=ALU.is_equal, fill=0.0, base=0,
                                       channel_multiplier=1)
                last = w.gp.affine_select(id_f[:], id_f[:], pattern=[[-1, 128]],
                                                  compare_op=ALU.is_equal, fill=0.0, base=0,
                                                  channel_multiplier=1)
                w.inc(last, "gq", 1)

                ST = [[pb4[0:1, 0:128], pb5[0:1, 0:128]],
                      [pbA[0:1, 0:128], pbA[0:1, 512:640]]]
                stats_done = [0] * 32
                sq_done = [0] * 32
                cp_done = [0] * 32
                norm_done = [0] * 32

                def stat_copies(w, i):
                    w.wait(w.ve, "pe", stats_done[i])
                    w.ve.tensor_scalar(mean_a[0:1, i * 128:(i + 1) * 128],
                                           ST[i % 2][0], 1.0 / 2048, None, ALU.mult)
                    ins = w.ve.tensor_scalar(msq_a[0:1, i * 128:(i + 1) * 128],
                                                     ST[i % 2][1], 1.0 / 2048, None, ALU.mult)
                    cp_done[i] = w.inc(ins, "dv", 1)

                for i in range(32):
                    bf = i % 2
                    sx = "x0" if bf == 0 else "x1"
                    w.wait(w.sy, "pe", stats_done[i - 2] if i >= 2 else 0)
                    w.inc(w.sy.dma_start(
                        out=xa[bf][:].rearrange("p (ec t) -> p ec t", t=128),
                        in_=d_xt[i].rearrange("(ec p) t -> p ec t", p=128)), sx, 16)
                    ld = c[sx]

                    w.wait(w.ve, sx, ld)
                    w.wait(w.ve, "pe", stats_done[i - 1] if i >= 1 else 0)
                    for ec in range(16):
                        ins = w.ve.tensor_mul(t1[bf][:, ec * 128:(ec + 1) * 128],
                                                      xa[bf][:, ec * 128:(ec + 1) * 128],
                                                      xa[bf][:, ec * 128:(ec + 1) * 128])
                    sq_done[i] = w.inc(ins, "dv", 1)
                    if i >= 1:
                        stat_copies(w, i - 1)

                    w.wait(w.te, sx, ld)
                    w.wait(w.te, "dv", sq_done[i])
                    w.wait(w.te, "dv", cp_done[i - 2] if i >= 2 else 0)
                    for ec in range(16):
                        w.te.matmul(ST[bf][0], ones_cf[:],
                                        xa[bf][:, ec * 128:(ec + 1) * 128],
                                        start=(ec == 0), stop=(ec == 15))
                    for ec in range(16):
                        ins = w.te.matmul(ST[bf][1], ones_cf[:],
                                                  t1[bf][:, ec * 128:(ec + 1) * 128],
                                                  start=(ec == 0), stop=(ec == 15))
                    stats_done[i] = w.inc(ins, "pe", 1)
                stat_copies(w, 31)

                w.ve.drain()
                w.ve.tensor_mul(var_a[:], mean_a[:], mean_a[:])
                w.ve.drain()
                w.ve.tensor_sub(var_a[:], msq_a[:], var_a[:])
                w.ve.drain()
                ins = w.ve.tensor_scalar(var_a[:], var_a[:], 1.0e-5, None, ALU.add)
                var_done = w.inc(ins, "dv", 1)
                w.wait(w.se, "dv", var_done)
                w.se.activation(var_a[:], var_a[:], AF.Ln)
                w.se.drain()
                ins = w.se.activation(var_a[:], var_a[:], AF.Exp, scale=-0.5)
                rstd_done = w.inc(ins, "ac", 1)
                rstd_a = var_a
                w.wait(w.ve, "ac", rstd_done)
                ins = w.ve.scalar_tensor_tensor(mean_a[:], mean_a[:], -1.0, rstd_a[:],
                                                        ALU.mult, ALU.mult)
                nmr_done = w.inc(ins, "dv", 1)
                nmr_a = mean_a

                BC = [[pbB[:, 0:128], pbB[:, 512:640]],
                      [pb4[:, 0:128], pb5[:, 0:128]]]
                bc_done = [0] * 32
                for i in range(32):
                    bf = i % 2
                    sx = "x0" if bf == 0 else "x1"
                    if i < 2:
                        w.wait(w.sy, "pe", stats_done[31])
                    w.wait(w.sy, "dv", norm_done[i - 2] if i >= 2 else 0)
                    w.inc(w.sy.dma_start(
                        out=xa[bf][:].rearrange("p (ec t) -> p ec t", t=128),
                        in_=d_xt[i].rearrange("(ec p) t -> p ec t", p=128)), sx, 16)
                    ld = c[sx]

                    w.wait(w.te, "ac", rstd_done)
                    w.wait(w.te, "dv", nmr_done)
                    w.wait(w.te, "dv", norm_done[i - 2] if i >= 2 else 0)
                    w.te.matmul(BC[bf][0], ones_rf[:], rstd_a[0:1, i * 128:(i + 1) * 128],
                                    start=True, stop=True)
                    ins = w.te.matmul(BC[bf][1], ones_rf[:], nmr_a[0:1, i * 128:(i + 1) * 128],
                                          start=True, stop=True)
                    bc_done[i] = w.inc(ins, "pe", 1)

                    w.wait(w.ve, "pe", bc_done[i])
                    w.wait(w.ve, sx, ld)
                    w.ve.tensor_copy(rstdb[bf][:], BC[bf][0])
                    w.ve.tensor_copy(nmrb[bf][:], BC[bf][1])
                    w.ve.drain()
                    ins = w.ve.tensor_mul(
                        t1[bf][:].rearrange("p (ec t) -> p ec t", t=128),
                        xa[bf][:].rearrange("p (ec t) -> p ec t", t=128),
                        rstdb[bf][:].rearrange("p (o t) -> p o t", o=1).to_broadcast([128, 16, 128]))
                    w.ve.drain()
                    ins = w.ve.tensor_add(
                        xn_sb[:].rearrange("p (ec t) -> p ec t", t=4096)[:, :, i * 128:(i + 1) * 128],
                        t1[bf][:].rearrange("p (ec t) -> p ec t", t=128),
                        nmrb[bf][:].rearrange("p (o t) -> p o t", o=1).to_broadcast([128, 16, 128]))
                    norm_done[i] = w.inc(ins, "dv", 1)

                w.bar()


            if "A" in PHASES:
                run_block(phaseA1)

        with ExitStack() as PA2:
            A = PA2.enter_context
            wbufA = [A(nc.sbuf_tensor(f"wbufA{i}", [128, 2048], BF16)) for i in range(2)]
            stg = [A(nc.sbuf_tensor(f"stg{i}", [128, 2048], BF16)) for i in range(2)]
            vtr = [A(nc.sbuf_tensor(f"vtr{i}", [128, 2048], BF16)) for i in range(2)]
            pbA = A(nc.psum_tensor("pbA_a2", [128, 1024], F32))
            pbB = A(nc.psum_tensor("pbB_a2", [128, 1024], F32))
            pb4 = A(nc.psum_tensor("pb4_a2", [128, 512], F32))
            pb5 = A(nc.psum_tensor("pb5_a2", [128, 512], F32))
            trb = [A(nc.psum_tensor(f"trb{i}", [128, 1024], BF16)) for i in range(2)]

            def phaseA2(w: Walk):
                c = w.cnt
                for i2 in range(0, 16, 4):
                    w.inc(w.gp.dma_start(out=b_wo[i2:i2 + 4], in_=d_wo[i2:i2 + 4]),
                          "pc", 16)
                for ec in range(16):
                    w.inc(w.gp.dma_start(out=d_xnsp[ec],
                                         in_=xn_sb[:, ec * 4096:(ec + 1) * 4096]),
                          "sp", 16)

                PSROT = [pbA[:, 0:512], pbA[:, 512:1024], pbB[:, 0:512],
                         pbB[:, 512:1024], pb4[:, 0:512], pb5[:, 0:512]]
                st_ = {
                    "wl_free": [0, 0],
                    "stg_guard": [("s0", 0), ("s1", 0)],
                    "tr_copied": [0, 0],
                    "vtr_sp": [0, 0],
                    "rot": 0,
                    "ev_slot": [0] * 6,
                }

                def proj(dw, bias_c, n_pass, spill_fn, transpose_v, pc_need):
                    for ps in range(n_pass):
                        for oc in range(8):
                                wbf = oc % 2
                                sw = "w0" if wbf == 0 else "w1"
                                w.wait(w.sy, "pe", st_["wl_free"][wbf])
                                w.wait(w.sy, "pc", pc_need)
                                w.inc(w.sy.dma_start(
                                    out=wbufA[wbf][:].rearrange("p (ec o) -> p ec o", o=128),
                                    in_=dw[oc].rearrange("(ec p) o -> p ec o", p=128)),
                                    sw, 16)
                                wld = c[sw]
                                w.wait(w.te, sw, wld)
                                gsem, gval = st_["stg_guard"][wbf]
                                w.wait(w.ve, gsem, gval)
                                for tq in range(4):
                                    slot = st_["rot"] % 6
                                    st_["rot"] += 1
                                    t0 = (ps * 4 + tq) * 512
                                    w.wait(w.te, "dv", st_["ev_slot"][slot])
                                    for ec in range(16):
                                        ins = w.te.matmul(
                                                PSROT[slot], wbufA[wbf][:, ec * 128:(ec + 1) * 128],
                                                xn_sb[:, ec * 4096 + t0: ec * 4096 + t0 + 512],
                                                start=(ec == 0), stop=(ec == 15))
                                    mm_done = w.inc(ins, "pe", 1)
                                    w.wait(w.ve, "pe", mm_done)
                                    ins = w.ve.tensor_scalar(
                                        stg[wbf][:, tq * 512:(tq + 1) * 512], PSROT[slot],
                                        bias_c[:, oc:oc + 1], None, ALU.add)
                                    st_["ev_slot"][slot] = w.inc(ins, "dv", 1)
                                st_["wl_free"][wbf] = c["pe"]
                                ev_done = c["dv"]

                                if transpose_v:
                                    sv = "v0" if wbf == 0 else "v1"
                                    w.wait(w.te, "dv", ev_done)
                                    w.wait(w.ve, sv, st_["vtr_sp"][wbf])
                                    for j in range(16):
                                        pj = j % 2
                                        w.wait(w.te, "dv", st_["tr_copied"][pj])
                                        ins = w.te.transpose(
                                                trb[pj][:, 0:128],
                                                stg[wbf][:, j * 128:(j + 1) * 128], id_bf[:])
                                        trd = w.inc(ins, "pe", 1)
                                        w.wait(w.ve, "pe", trd)
                                        ins = w.ve.tensor_copy(
                                                vtr[wbf][:, j * 128:(j + 1) * 128],
                                                trb[pj][:, 0:128])
                                        st_["tr_copied"][pj] = w.inc(ins, "dv", 1)
                                    st_["stg_guard"][wbf] = ("pe", c["pe"])
                                    w.wait(w.gp, "dv", st_["tr_copied"][0])
                                    w.wait(w.gp, "dv", st_["tr_copied"][1])
                                    w.inc(w.gp.dma_start(
                                        out=d_vsp[oc, ps], in_=vtr[wbf][:]), sv, 16)
                                    st_["vtr_sp"][wbf] = c[sv]
                                else:
                                    ss = "s0" if wbf == 0 else "s1"
                                    w.wait(w.gp, "dv", ev_done)
                                    w.inc(w.gp.dma_start(out=spill_fn(ps, oc), in_=stg[wbf][:]),
                                          ss, 16)
                                    st_["stg_guard"][wbf] = (ss, c[ss])

                proj(b_wq, bq_c, 2,
                     lambda ps, oc: d_qsp[oc][:, ps * 2048:(ps + 1) * 2048], False, 2 * 16)
                proj(b_wk, bk_c, 2,
                     lambda ps, oc: d_ksp[oc][:, ps * 2048:(ps + 1) * 2048], False, 4 * 16)
                proj(b_wv, bv_c, 2, None, True, 6 * 16)

                w.bar()


            if "A" in PHASES:
                run_block(phaseA2)


    with ExitStack() as PB:
        A = PB.enter_context
        q_sb = A(nc.sbuf_tensor("q_sb", [128, 4096], BF16))
        b_ch = [A(nc.sbuf_tensor(f"b_ch{i}", [128, 1024], F32)) for i in range(2)]
        k_sb = [A(nc.sbuf_tensor(f"k_sb{i}", [128, 4096], BF16)) for i in range(2)]
        v_sb = [A(nc.sbuf_tensor(f"v_sb{i}", [128, 4096], BF16)) for i in range(2)]
        ex_sb = [A(nc.sbuf_tensor(f"ex_sb{i}", [128, 1024], BF16)) for i in range(2)]
        rc_sb = A(nc.sbuf_tensor("rc_sb", [1, 512], F32))
        bc_sb = A(nc.sbuf_tensor("bc_sb", [128, 512], F32))
        avn = [A(nc.sbuf_tensor(f"avn{i}", [128, 512], BF16)) for i in range(2)]
        pbA = A(nc.psum_tensor("pbA_b", [128, 1024], F32))
        pbB = A(nc.psum_tensor("pbB_b", [128, 1024], F32))
        pb4 = A(nc.psum_tensor("pb4_b", [128, 512], F32))
        pb5 = A(nc.psum_tensor("pb5_b", [128, 512], F32))
        pb6 = A(nc.psum_tensor("pb6_b", [128, 512], F32))
        pb7 = A(nc.psum_tensor("pb7_b", [128, 512], F32))

        def phaseB(w: Walk):
            c = w.cnt
            SC = [pbA, pbB]
            AV = [pb4, pb6]
            DN = [pb5, pb7]
            kv_free = [0, 0]
            avn_sp = [0, 0]
            ex_free = [0, 0]
            sc_free = [0, 0]
            b_free = [0, 0]
            b_ld = {}
            ep_pend = None
            ep_av_free = [0, 0]
            gi = 0
            for h in range(8):
                hb = h % 2
                skv = "kv0" if hb == 0 else "kv1"
                w.wait(w.sy, "pe", kv_free[hb])
                w.inc(w.sy.dma_start(out=k_sb[hb][:], in_=d_ksp[h]), skv, 16)
                w.inc(w.sy.dma_start(out=v_sb[hb][:, 0:2048], in_=d_vsp[h, 0]),
                      skv, 16)
                w.inc(w.sy.dma_start(out=v_sb[hb][:, 2048:4096], in_=d_vsp[h, 1]),
                      skv, 16)
                w.inc(w.sy.dma_start(out=q_sb[:], in_=d_qsp[h]), skv, 16)
                kv_ld = c[skv]
                w.wait(w.te, skv, kv_ld)

                for qt in range(8):
                    qp = qt % 2
                    n_kp = 2 * (qt + 1)
                    n_kt = 4 * (qt + 1)
                    sc_done = [0] * n_kp
                    exp_done = [0] * n_kp
                    for kp in range(n_kp + 1):
                        pp = kp % 2
                        if kp < n_kp:
                            bb = kp % 2
                            sb2 = "x0" if bb == 0 else "x1"
                            w.wait(w.sy, "dv", b_free[bb])
                            w.inc(w.sy.dma_start(
                                out=b_ch[bb][:].rearrange("p (u f) -> p u f", f=512),
                                in_=d_bt[qt, 2 * kp:2 * kp + 2].transpose([1, 0, 2])),
                                sb2, 16)
                            b_ld[(qt, kp)] = c[sb2]
                            w.wait(w.te, "ac", sc_free[pp])
                            for u in range(2):
                                kt = 2 * kp + u
                                ins = w.te.matmul(
                                    SC[pp][:, u * 512:(u + 1) * 512],
                                    k_sb[hb][:, kt * 128:(kt + 1) * 128],
                                    q_sb[:, qt * 512:(qt + 1) * 512],
                                    start=True, stop=True)
                            sc_done[kp] = w.inc(ins, "pe", 1)
                        if kp == 1 and ep_pend is not None:
                            eqp, erc = ep_pend
                            w.wait(w.te, "dv", erc)
                            ins = w.te.matmul(DN[eqp][:, 0:512], ones_rf[:],
                                              rc_sb[:], start=True, stop=True)
                            epb = w.inc(ins, "pe", 1)
                            w.wait(w.ve, "pe", epb)
                            w.ve.tensor_copy(bc_sb[:], DN[eqp][:, 0:512])
                            w.ve.drain()
                            sa = "a0" if eqp == 0 else "a1"
                            w.wait(w.ve, sa, avn_sp[eqp])
                            ins = w.ve.tensor_mul(avn[eqp][:], AV[eqp][:, 0:512],
                                                  bc_sb[:])
                            avn_done = w.inc(ins, "dv", 1)
                            ep_av_free[eqp] = avn_done
                            w.wait(w.gp, "dv", avn_done)
                            w.inc(w.gp.dma_start(out=d_avsp[ep_qt, ep_h],
                                                 in_=avn[eqp][:]), sa, 16)
                            avn_sp[eqp] = c[sa]
                            if gi < 8:
                                w.inc(w.gp.dma_start(out=b_w1[gi], in_=d_w1[gi]),
                                      "pc", 16)
                            elif gi < 16:
                                w.inc(w.gp.dma_start(out=b_w2[gi - 8],
                                                     in_=d_w2[gi - 8]), "pc", 16)
                            gi += 1
                            ep_pend = None
                        if kp >= 1:
                            km = kp - 1
                            pm = km % 2
                            w.wait(w.te, "ac", exp_done[km])
                            if km == 0:
                                w.wait(w.te, "dv", ep_av_free[qp])
                            for u in range(2):
                                kt = 2 * km + u
                                w.te.matmul(DN[qp][0:1, 0:512], ones_cb[:],
                                            ex_sb[pm][:, u * 512:(u + 1) * 512],
                                            start=(kt == 0), stop=(kt == n_kt - 1))
                                ins = w.te.matmul(AV[qp][:, 0:512],
                                                  v_sb[hb][:, kt * 128:(kt + 1) * 128],
                                                  ex_sb[pm][:, u * 512:(u + 1) * 512],
                                                  start=(kt == 0), stop=(kt == n_kt - 1))
                            ex_free[pm] = w.inc(ins, "pe", 1)
                        if kp < n_kp:
                            bb = kp % 2
                            sb2 = "x0" if bb == 0 else "x1"
                            w.wait(w.ve, "pe", sc_done[kp])
                            w.wait(w.ve, sb2, b_ld[(qt, kp)])
                            ins = w.ve.scalar_tensor_tensor(
                                SC[pp][:], b_ch[bb][:], slp_c[:, h:h + 1],
                                SC[pp][:], ALU.mult, ALU.add)
                            bias_done = w.inc(ins, "dv", 1)
                            b_free[bb] = bias_done
                            w.wait(w.se, "dv", bias_done)
                            w.wait(w.se, "pe", ex_free[pp])
                            ins = w.se.activation(ex_sb[pp][:], SC[pp][:], AF.Exp,
                                                  scale=1.0 / RD)
                            exp_done[kp] = w.inc(ins, "ac", 1)
                            sc_free[pp] = exp_done[kp]
                    kv_free[hb] = c["pe"]

                    w.wait(w.ve, "pe", c["pe"])
                    ins = w.ve.reciprocal(rc_sb[:], DN[qp][0:1, 0:512])
                    rc_done = w.inc(ins, "dv", 1)
                    ep_pend = (qp, rc_done)
                    ep_hb = hb
                    ep_qt, ep_h = qt, h

            eqp, erc = ep_pend
            w.wait(w.te, "dv", erc)
            ins = w.te.matmul(DN[eqp][:, 0:512], ones_rf[:], rc_sb[:],
                              start=True, stop=True)
            epb = w.inc(ins, "pe", 1)
            w.wait(w.ve, "pe", epb)
            w.ve.tensor_copy(bc_sb[:], DN[eqp][:, 0:512])
            w.ve.drain()
            sa = "a0" if eqp == 0 else "a1"
            w.wait(w.ve, sa, avn_sp[eqp])
            ins = w.ve.tensor_mul(avn[eqp][:], AV[eqp][:, 0:512], bc_sb[:])
            avn_done = w.inc(ins, "dv", 1)
            w.wait(w.gp, "dv", avn_done)
            w.inc(w.gp.dma_start(out=d_avsp[ep_qt, ep_h], in_=avn[eqp][:]),
                  sa, 16)
            avn_sp[eqp] = c[sa]

            w.wait(w.gp, "a0", c["a0"])
            w.wait(w.gp, "a1", c["a1"])
            for qt in range(8):
                ins = w.gp.collective_compute(
                    "AllGather", ALU.bypass,
                    replica_groups=[[0, 1, 2, 3, 4, 5, 6, 7]],
                    ins=[d_avsp[qt].rearrange("h p t -> (h p) t")],
                    outs=[d_avag[qt]])
                w.inc(ins, "ag", 1)
            w.bar(skip=("ag",))

        if "B" in PHASES:
            run_block(phaseB)

    with ExitStack() as PC:
        A = PC.enter_context
        xn_c = A(nc.sbuf_tensor("xn_c", [128, 16384], BF16))
        av_c = A(nc.sbuf_tensor("av_c", [128, 16384], BF16))
        acc = A(nc.sbuf_tensor("acc", [128, 16384], F32))
        act_sb = A(nc.sbuf_tensor("act_sb", [128, 11264], BF16))
        wbufC = [A(nc.sbuf_tensor(f"wbufC{i}", [128, 2048], BF16)) for i in range(2)]
        wubuf = [A(nc.sbuf_tensor(f"wubuf{i}", [128, 2048], BF16)) for i in range(2)]
        sg_sb = [A(nc.sbuf_tensor(f"sg_sb{i}", [128, 512], F32)) for i in range(2)]
        cb_sb = A(nc.sbuf_tensor("cb_sb", [128, 1024], F32))
        tmp_sb = [A(nc.sbuf_tensor(f"tmp_sb{i}", [128, 512], F32)) for i in range(2)]
        xr_sb = [A(nc.sbuf_tensor(f"xr_sb{i}", [128, 2048], F32)) for i in range(2)]
        cmbT = A(nc.sbuf_tensor("cmbT", [8, 1024], F32))
        lgt = A(nc.sbuf_tensor("lgt", [128, 8], F32))
        negm = A(nc.sbuf_tensor("negm", [128, 1], F32))
        egl = A(nc.sbuf_tensor("egl", [128, 8], F32))
        ssum = A(nc.sbuf_tensor("ssum", [128, 1], F32))
        rsu = A(nc.sbuf_tensor("rsu", [128, 1], F32))
        p8 = A(nc.sbuf_tensor("p8", [128, 8], F32))
        gemask = A(nc.sbuf_tensor("gemask", [128, 8], F32))
        m12 = A(nc.sbuf_tensor("m12", [128, 1], F32))
        cmb = A(nc.sbuf_tensor("cmb", [128, 8], F32))
        pbA = A(nc.psum_tensor("pbA_c", [128, 1024], F32))
        pbB = A(nc.psum_tensor("pbB_c", [128, 1024], F32))
        pb4 = A(nc.psum_tensor("pb4_c", [128, 512], F32))
        pb5 = A(nc.psum_tensor("pb5_c", [128, 512], F32))
        pb6 = A(nc.psum_tensor("pb6_c", [128, 512], F32))
        pb7 = A(nc.psum_tensor("pb7_c", [128, 512], F32))

        def phaseC(w: Walk):
            c = w.cnt
            s = {
                "wl_free": [0, 0],
                "wu_free": [0, 0],
                "wo_ev": [0, 0],
                "wo_last": 0,
                "o_used": [0, 0],
                "o_last": 0,
                "xr_used": [0, 0],
                "gate_dv": 0,
                "sg_done": 0,
                "act_done": 0,
                "sg_par": [0, 0],
                "act_par": [0, 0],
                "cb_used": 0,
                "w2_all_prev": 0,
                "st_prev": 0,
                "th_pe_prev": 0,
            }
            d_avag4 = d_avag.rearrange("q (dc p) t -> q dc p t", p=128)
            for th in range(2):
                t0 = th * 1024
                w.wait(w.sy, "pe", s["th_pe_prev"])
                w.sw_dma_sy("cl", lambda eng, lg, t0=t0: eng.dma_start(
                    out=xn_c[:].rearrange("p (ec t) -> p ec t", t=1024),
                    in_=d_xnsp[:, :, (lg % 2) * 2048 + t0: (lg % 2) * 2048 + t0 + 1024]
                        .transpose([1, 0, 2])))
                cl_xn = c["cl"]
                for tc in (range(8) if "G" in CPARTS else []):
                    w.wait(w.te, "in0", 112)
                    w.wait(w.te, "cl", cl_xn)
                    w.wait(w.te, "dv", s["gate_dv"])
                    w.wait(w.te, "dv", s["o_last"])
                    w.wait(w.te, "dv", s["act_done"])
                    for ec in range(16):
                        ins = w.te.matmul(
                            pb5[:, 0:8],
                            xn_c[:, ec * 1024 + tc * 128: ec * 1024 + (tc + 1) * 128],
                            gate_sb[:, ec * 8:(ec + 1) * 8],
                            start=(ec == 0), stop=(ec == 15))
                    lg_done = w.inc(ins, "pe", 1)

                    w.wait(w.ve, "pe", lg_done)
                    w.ve.tensor_copy(lgt[:], pb5[:, 0:8])
                    w.ve.drain()
                    ins = w.ve.reduce_max(negm[:], lgt[:], axis=AX.X, negate=True)
                    nm_done = w.inc(ins, "dv", 1)
                    w.wait(w.se, "dv", nm_done)
                    ins = w.se.activation(egl[:], lgt[:], AF.Exp, bias=negm[:])
                    eg_done = w.inc(ins, "ac", 1)
                    w.wait(w.ve, "ac", eg_done)
                    w.ve.reduce_sum(ssum[:], egl[:], axis=AX.X)
                    w.ve.drain()
                    w.ve.reciprocal(rsu[:], ssum[:])
                    w.ve.drain()
                    w.ve.tensor_scalar(cmb[:], egl[:], rsu[:], None, ALU.mult)
                    w.ve.drain()
                    w.ve.max(out=p8[:], in_=cmb[:])
                    w.ve.drain()
                    w.ve.tensor_scalar(gemask[:], cmb[:], p8[:, 1:2], None, ALU.is_ge)
                    w.ve.tensor_add(m12[:], p8[:, 0:1], p8[:, 1:2])
                    w.ve.drain()
                    w.ve.reciprocal(m12[:], m12[:])
                    w.ve.drain()
                    w.ve.tensor_scalar(cmb[:], cmb[:], m12[:], None, ALU.mult)
                    w.ve.drain()
                    ins = w.ve.tensor_mul(cmb[:], cmb[:], gemask[:])
                    cmb_done = w.inc(ins, "dv", 1)

                    w.wait(w.te, "dv", cmb_done)
                    w.wait(w.te, "gq", 1)
                    ins = w.te.transpose(pb4[0:8, 0:128], cmb[:], id_f[:])
                    tr_done = w.inc(ins, "pe", 1)
                    w.wait(w.ve, "pe", tr_done)
                    ins = w.ve.tensor_copy(cmbT[:, tc * 128:(tc + 1) * 128],
                                           pb4[0:8, 0:128])
                    s["gate_dv"] = w.inc(ins, "dv", 1)

                PSGP = [[pbB[:, 0:512], pbB[:, 512:1024]], [pb4[:, 0:512], pb5[:, 0:512]]]
                PSUP = [[pbA[:, 0:512], pbA[:, 512:1024]], [pb6[:, 0:512], pb7[:, 0:512]]]
                PSOP = [[pb4[:, 0:512], pb5[:, 0:512]], [pb6[:, 0:512], pb7[:, 0:512]]]
                for e in (range(EXP_N) if "E" in CPARTS else []):
                    w.wait(w.te, "dv", s["gate_dv"])
                    w.wait(w.te, "dv", s["cb_used"])
                    w.wait(w.te, "dv", s["wo_last"])
                    w.wait(w.te, "ac", s["sg_done"])
                    w.wait(w.te, "dv", s["act_done"])
                    for ts in range(2):
                        if ts == 1:
                            w.wait(w.te, "dv", c["dv"])
                        ins = w.te.matmul(pb6[:, 0:512],
                                          sel_sb[:, e * 128:(e + 1) * 128],
                                          cmbT[:, ts * 512:(ts + 1) * 512],
                                          start=True, stop=True)
                        cbb = w.inc(ins, "pe", 1)
                        w.wait(w.ve, "pe", cbb)
                        ins = w.ve.tensor_copy(cb_sb[:, ts * 512:(ts + 1) * 512],
                                               pb6[:, 0:512])
                        w.inc(ins, "dv", 1)
                    cb_ready = c["dv"]

                    for j in (range(11) if EXP_STAGE in ("w1", "full") else []):
                        jb = j % 2
                        swg = "w0" if jb == 0 else "w1"
                        swu = "w2" if jb == 0 else "w3"
                        w.wait(w.sy, "pe", s["wl_free"][jb])
                        w.inc(w.sy.dma_start(
                            out=wbufC[jb][:].rearrange("p (ec o) -> p ec o", o=128),
                            in_=b_w1[e, j].rearrange("(ec p) o -> p ec o", p=128)),
                            swg, 16)
                        w.wait(w.sy, "pe", s["wu_free"][jb])
                        w.inc(w.sy.dma_start(
                            out=wubuf[jb][:].rearrange("p (ec o) -> p ec o", o=128),
                            in_=b_w1[e, j + 11].rearrange("(ec p) o -> p ec o", p=128)),
                            swu, 16)
                        wldg, wldu = c[swg], c[swu]

                        jp = j % 2
                        PSG, PSU = PSGP[jp], PSUP[jp]
                        w.wait(w.te, swg, wldg)
                        w.wait(w.te, swu, wldu)
                        w.wait(w.te, "ac", s["sg_par"][jp])
                        w.wait(w.te, "dv", s["act_par"][jp])
                        w.wait(w.te, "dv", cb_ready)
                        for ts in range(2):
                            for ec in range(16):
                                w.te.matmul(
                                    PSG[ts], wbufC[jb][:, ec * 128:(ec + 1) * 128],
                                    xn_c[:, ec * 1024 + ts * 512: ec * 1024 + ts * 512 + 512],
                                    start=(ec == 0), stop=(ec == 15))
                        for ts in range(2):
                            for ec in range(16):
                                ins = w.te.matmul(
                                    PSU[ts], wubuf[jb][:, ec * 128:(ec + 1) * 128],
                                    xn_c[:, ec * 1024 + ts * 512: ec * 1024 + ts * 512 + 512],
                                    start=(ec == 0), stop=(ec == 15))
                        w1_done = w.inc(ins, "pe", 1)
                        s["wl_free"][jb] = c["pe"]
                        s["wu_free"][jb] = c["pe"]

                        w.wait(w.se, "pe", w1_done)
                        w.wait(w.se, "dv", s["act_done"])
                        for ts in range(2):
                            ins = w.se.activation(sg_sb[ts][:], PSG[ts], AF.Silu)
                        s["sg_done"] = w.inc(ins, "ac", 1)
                        s["sg_par"][jp] = s["sg_done"]
                        w.wait(w.ve, "ac", s["sg_done"])
                        w.wait(w.ve, "pe", s["w2_all_prev"])
                        for ts in range(2):
                            ins = w.ve.tensor_mul(
                                act_sb[:, j * 1024 + ts * 512: j * 1024 + ts * 512 + 512],
                                sg_sb[ts][:], PSU[ts])
                        s["act_done"] = w.inc(ins, "dv", 1)
                        s["act_par"][jp] = s["act_done"]

                    for hc in (range(16) if EXP_STAGE == "full" else []):
                        hb2 = hc % 2
                        sw = "w0" if hb2 == 0 else "w1"
                        w.wait(w.sy, "pe", s["wl_free"][hb2])
                        w.inc(w.sy.dma_start(
                            out=wbufC[hb2][:, 0:1408].rearrange("p (j o) -> p j o", o=128),
                            in_=b_w2[e, hc].rearrange("(j p) o -> p j o", p=128)),
                            sw, 16)
                        wld = c[sw]
                        PSO = PSOP[hb2]
                        w.wait(w.te, sw, wld)
                        w.wait(w.te, "dv", s["act_done"])
                        w.wait(w.te, "ac", s["sg_done"])
                        w.wait(w.te, "dv", s["o_used"][hb2])
                        for ts in range(2):
                            for j in range(11):
                                ins = w.te.matmul(
                                    PSO[ts], wbufC[hb2][:, j * 128:(j + 1) * 128],
                                    act_sb[:, j * 1024 + ts * 512: j * 1024 + ts * 512 + 512],
                                    start=(j == 0), stop=(j == 10))
                        w2_done = w.inc(ins, "pe", 1)
                        s["wl_free"][hb2] = c["pe"]

                        w.wait(w.ve, "pe", w2_done)
                        w.wait(w.ve, "dv", cb_ready)
                        if e == 0:
                            w.wait(w.ve, "st", s["st_prev"])
                            for ts in range(2):
                                a0 = hc * 1024 + ts * 512
                                ins = w.ve.tensor_mul(
                                    acc[:, a0:a0 + 512], PSO[ts],
                                    cb_sb[:, ts * 512:(ts + 1) * 512])
                        else:
                            for ts in range(2):
                                w.ve.tensor_mul(tmp_sb[ts][:], PSO[ts],
                                                cb_sb[:, ts * 512:(ts + 1) * 512])
                            w.ve.drain()
                            for ts in range(2):
                                a0 = hc * 1024 + ts * 512
                                ins = w.ve.tensor_add(acc[:, a0:a0 + 512],
                                                      acc[:, a0:a0 + 512],
                                                      tmp_sb[ts][:])
                        s["o_used"][hb2] = w.inc(ins, "dv", 1)
                        s["o_last"] = s["o_used"][hb2]
                        w.ve.drain()
                    s["cb_used"] = c["dv"]
                    s["w2_all_prev"] = c["pe"]

                w.wait(w.sy, "ag", 6 if th == 0 else 8)
                for qq in range(2):
                    w.sw_dma_sy("cl", lambda eng, lg, th=th, qq=qq: eng.dma_start(
                        out=av_c[:].rearrange("p (ec t) -> p ec t", t=1024)
                            [:, :, qq * 512:(qq + 1) * 512],
                        in_=d_avag4[(lg % 2) * 4 + th * 2 + qq,
                                    (lg % 8 // 2) * 16:(lg % 8 // 2) * 16 + 16]
                            .transpose([1, 0, 2])))
                cl_av = c["cl"]
                PSWP = [[pbA[:, 0:512], pbA[:, 512:1024]],
                        [pbB[:, 0:512], pbB[:, 512:1024]]]
                for oc in (range(16) if "W" in CPARTS else []):
                    wbf = oc % 2
                    sw = "w0" if wbf == 0 else "w1"
                    w.wait(w.sy, "pe", s["wl_free"][wbf])
                    w.wait(w.sy, "pc", 26 * 16)
                    w.inc(w.sy.dma_start(
                        out=wbufC[wbf][:].rearrange("p (ec o) -> p ec o", o=128),
                        in_=b_wo[oc].rearrange("(ec p) o -> p ec o", p=128)), sw, 16)
                    wld = c[sw]
                    PSW = PSWP[wbf]
                    w.wait(w.te, sw, wld)
                    w.wait(w.te, "cl", cl_av)
                    w.wait(w.te, "dv", s["wo_ev"][wbf])
                    w.wait(w.te, "ac", s["sg_done"])
                    w.wait(w.te, "dv", s["act_done"])
                    for ec in range(16):
                        for tq in range(2):
                            ins = w.te.matmul(
                                PSW[tq], wbufC[wbf][:, ec * 128:(ec + 1) * 128],
                                av_c[:, ec * 1024 + tq * 512: ec * 1024 + tq * 512 + 512],
                                start=(ec == 0), stop=(ec == 15))
                    mm_done = w.inc(ins, "pe", 1)
                    s["wl_free"][wbf] = c["pe"]
                    w.wait(w.ve, "pe", mm_done)
                    for tq in range(2):
                        a0 = oc * 1024 + tq * 512
                        ins = w.ve.scalar_tensor_tensor(
                            acc[:, a0:a0 + 512], PSW[tq], bo_c[:, oc:oc + 1],
                            acc[:, a0:a0 + 512], ALU.add, ALU.add)
                        w.ve.drain()
                    s["wo_ev"][wbf] = w.inc(ins, "dv", 1)
                    s["wo_last"] = s["wo_ev"][wbf]

                for qq in (range(8) if "R" in CPARTS else []):
                    qb2 = qq % 2
                    sx = "xr0" if qb2 == 0 else "xr1"
                    w.wait(w.sy, "dv", s["xr_used"][qb2])
                    w.inc(w.sy.dma_start(
                        out=xr_sb[qb2][:].rearrange("p (ec t) -> p ec t", t=128),
                        in_=d_xo[8 * th + qq].rearrange("(ec p) t -> p ec t", p=128)),
                        sx, 16)
                    xld = c[sx]
                    w.wait(w.ve, sx, xld)
                    if qq == 0:
                        w.ve.drain()
                    acc3 = acc[:].rearrange("p (oc t) -> p oc t", t=1024)
                    xr3 = xr_sb[qb2][:].rearrange("p (ec t) -> p ec t", t=128)
                    ins = w.ve.tensor_add(
                        acc3[:, :, qq * 128:(qq + 1) * 128],
                        acc3[:, :, qq * 128:(qq + 1) * 128], xr3)
                    s["xr_used"][qb2] = w.inc(ins, "dv", 1)
                    w.ve.drain()
                res_done = c["dv"]
                w.wait(w.sy, "dv", res_done)
                w.inc(w.sy.dma_start(out=d_y[th], in_=acc[:]), "st", 16)
                s["st_prev"] = c["st"]
                s["th_pe_prev"] = c["pe"]

            w.wait(w.sy, "st", c["st"])
            w.bar()

        if "C" in PHASES:
            run_block(phaseC)

    ES.close()
    return nc



def _pack_shared(inp):
    f32 = np.float32
    Wq = np.asarray(inp["L0_attn_Wq"], f32)
    Wk = np.asarray(inp["L0_attn_Wk"], f32)
    Wv = np.asarray(inp["L0_attn_Wv"], f32)
    Wo = np.asarray(inp["L0_attn_Wo"], f32)
    w1 = np.asarray(inp["L0_ffn_w1"], f32)
    w2 = np.asarray(inp["L0_ffn_w2"], f32)
    gw = np.asarray(inp["L0_ffn_gate_w"], f32)

    def blkT(W):
        return np.ascontiguousarray(
            W.T.reshape(E, W.shape[0] // 128, 128).transpose(1, 0, 2))

    qg = np.arange(4096)
    kg = np.arange(4096)
    M = np.where(kg[:, None] <= qg[None, :],
                 (kg[:, None] - qg[None, :]).astype(f32), f32(NEG))
    btab = np.ascontiguousarray(M.reshape(32, 128, 8, 512).transpose(2, 0, 1, 3))

    return {
        "_wqf": blkT(Wq), "_wkf": blkT(Wk), "_wvf": blkT(Wv), "wo": blkT(Wo),
        "btab": btab,
        "w1": np.ascontiguousarray(
            w1.transpose(0, 2, 1).reshape(8, E, 22, 128).transpose(0, 2, 1, 3)),
        "w2": np.ascontiguousarray(
            w2.transpose(0, 2, 1).reshape(8, NI, 16, 128).transpose(0, 2, 1, 3)),
        "gate": np.ascontiguousarray(
            gw.T.reshape(16, 128, 8).transpose(1, 0, 2).reshape(128, 128)),
        "_bqf": np.ascontiguousarray(np.asarray(inp["L0_attn_bq"], f32).reshape(16, 128).T),
        "_bkf": np.ascontiguousarray(np.asarray(inp["L0_attn_bk"], f32).reshape(16, 128).T),
        "_bvf": np.ascontiguousarray(np.asarray(inp["L0_attn_bv"], f32).reshape(16, 128).T),
        "bo": np.ascontiguousarray(np.asarray(inp["L0_attn_bo"], f32).reshape(16, 128).T),
        "sel8": np.ascontiguousarray(np.repeat(np.eye(8, dtype=f32), 128, axis=1)),
    }


def _pack_core(inp, core, shared):
    f32 = np.float32
    b, hg = core // 2, core % 2
    x = np.asarray(inp["L0_attn_x"], f32)[b]
    xT = np.ascontiguousarray(x.T)
    xt = np.ascontiguousarray(xT.reshape(E, 32, 128).transpose(1, 0, 2))
    hs = slice(hg * 8, (hg + 1) * 8)
    slp = np.broadcast_to(
        np.asarray([SLOPES[hg * 8 + i] * RD for i in range(8)], f32)[None, :],
        (128, 8))
    d = {
        "xt": xt,
        "xo": np.ascontiguousarray(xt[hg * 16:(hg + 1) * 16]),
        "wq": np.ascontiguousarray(shared["_wqf"][hs]),
        "wk": np.ascontiguousarray(shared["_wkf"][hs]),
        "wv": np.ascontiguousarray(shared["_wvf"][hs]),
        "bq": np.ascontiguousarray(shared["_bqf"][:, hs]),
        "bk": np.ascontiguousarray(shared["_bkf"][:, hs]),
        "bv": np.ascontiguousarray(shared["_bvf"][:, hs]),
        "slp": np.ascontiguousarray(slp),
    }
    for k2, v2 in shared.items():
        if not k2.startswith("_"):
            d[k2] = v2
    return d


def kernel(**inputs):
    nc = build_program()
    shared = _pack_shared(inputs)
    in_maps = [_pack_core(inputs, cc, shared) for cc in range(8)]
    res = run_bass_kernel_spmd(nc, in_maps, core_ids=list(range(8)))
    out = np.zeros((4, 4096, 2048), dtype=np.float32)
    for cc in range(8):
        b, h = cc // 2, cc % 2
        y = np.asarray(res.results[cc]["y"])
        blk = y.reshape(2, 128, 16, 1024).transpose(0, 3, 2, 1).reshape(2048, 2048)
        out[b, h * 2048:(h + 1) * 2048, :] = blk
    return out
